# version 02
# baseline (speedup 1.0000x reference)
import contextlib
import math
import numpy as np
import concourse.bass as bass
import concourse.mybir as mybir
from concourse.bass_utils import run_bass_kernel_spmd

F32 = mybir.dt.float32
BF16 = mybir.dt.bfloat16
AF = mybir.ActivationFunctionType
ALU = mybir.AluOpType
AX = mybir.AxisListType

NCORES = 8
TLOC = 2048
NTC = TLOC // 128
D = 2048
NDC = D // 128
SEQ = 4096
BATCH = 4

ENGS = ("t", "v", "s", "g", "y")


class Sem:
    def __init__(self, ctx, name):
        self.ctx = ctx
        self.h = ctx.stack.enter_context(ctx.nc.semaphore(name))
        self.n = 0
    def inc(self, instr, k=1):
        instr.then_inc(self.h, k)
        self.ctx._note_used(instr)
        self.n += k
        tp = getattr(self.ctx, "_tiny_pending", None)
        if tp is not None:
            for e, ins2 in tp.items():
                if ins2 is instr:
                    self.ctx._tiny_guard[e] = (self.h, self.n)
                    tp[e] = None
        return instr
    def wait(self, eng, val=None):
        eng.wait_ge(self.h, self.n if val is None else val)


class Bank:
    def __init__(self, ctx, i):
        self.ctx = ctx
        self.i = i
        self.ap = ctx.stack.enter_context(ctx.nc.psum_tensor(f"ps{i}", [128, 512], F32))
        self.sw = Sem(ctx, f"ps{i}w")
        self.sr = Sem(ctx, f"ps{i}r")
    def begin(self, eng):
        self.sr.wait(eng)
    def fin_write(self, instr):
        self.sw.inc(instr)
    def begin_read(self, eng_letter_or_proxy):
        if isinstance(eng_letter_or_proxy, str):
            self._reader = eng_letter_or_proxy
        else:
            for e, pr in self.ctx.eng.items():
                if pr is eng_letter_or_proxy:
                    self._reader = e
                    break
        self.sw.wait(self.ctx.eng[self._reader])
    def fin_read(self, instr=None):
        self.sr.inc(self.ctx.tiny(self._reader))


class EngProxy:
    def __init__(self, eng):
        self._eng = eng
        self.last = None
        self.last_used = False
        self._auto_fence = True
        self._ctx = None
        self._letter = None
    def __getattr__(self, name):
        f = getattr(self._eng, name)
        if not callable(f):
            return f
        def wrap(*a, **kw):
            r = f(*a, **kw)
            if r is not None and hasattr(r, "then_inc") and name not in ("wait_ge", "nop"):
                self.last = r
                self.last_used = False
                if (self._auto_fence and self._ctx is not None
                        and self._letter in ("v", "s", "g")
                        and not name.startswith("dma")
                        and not name.startswith("indirect")
                        and not name.startswith("collective")
                        and not name.startswith("remote")):
                    self._ctx.fence(self._letter)
            return r
        return wrap


class Ctx:
    def __init__(self, nc, stack):
        self.nc = nc
        self.stack = stack
        self.t = EngProxy(nc.tensor)
        self.v = EngProxy(nc.vector)
        self.s = EngProxy(nc.scalar)
        self.g = EngProxy(nc.gpsimd)
        self.y = EngProxy(nc.sync)
        self.eng = {"t": self.t, "v": self.v, "s": self.s, "g": self.g, "y": self.y}
        for _e, _p in self.eng.items():
            _p._ctx = self
            _p._letter = _e
        self.banks = [Bank(self, i) for i in range(8)]
        self.rrb = 0
        self.tick = {e: Sem(self, f"tick_{e}") for e in ENGS}
        self.dsem = {}
        self._sbufs = {}
        self._alias = {}
        self._sb_bytes = 0
        self._tiny_pending = {e: None for e in ENGS}
        self._tiny_guard = {e: None for e in ENGS}
        self.ln_off = (36864, 34816, 32768)

    def _note_used(self, instr):
        for e in ENGS:
            pr = self.eng[e]
            if pr.last is instr:
                pr.last_used = True

    def tiny(self, e):
        eng = self.eng[e]
        saved, eng._auto_fence = eng._auto_fence, False
        if e in ("t", "y"):
            eng._auto_fence = saved
            return eng.nop()
        tk = self.sbuf(f"tk_{e}", [1, 2])
        g = self._tiny_guard[e]
        if g is not None:
            eng.wait_ge(g[0], g[1])
        ins = eng.memzero(tk[0:1, 0:1]) if e == "s" else eng.memset(tk[0:1, 0:1], 0.0)
        self._tiny_pending[e] = ins
        eng._auto_fence = saved
        return ins

    def gsem(self, name):
        name = self._canon(name)
        if not hasattr(self, "_gsems"):
            self._gsems = {}
        if name not in self._gsems:
            self._gsems[name] = Sem(self, name)
        return self._gsems[name]

    def t_gate(self, e):
        w = self.eng[e]
        for b in self.banks:
            if b.sw.n:
                b.sw.wait(w)

    def mark(self, e):
        return self.tick[e].inc(self.tiny(e))

    def fence(self, e):
        ins = self.eng[e].last
        sem = self.tick[e]
        if ins is None or self.eng[e].last_used:
            self.mark(e)
            self.eng[e].wait_ge(sem.h, sem.n)
            return
        sem.inc(ins)
        self.eng[e].wait_ge(sem.h, sem.n)

    import contextlib as _cl

    @_cl.contextmanager
    def fenced(self, *engs):
        saved = {e: self.eng[e]._auto_fence for e in engs}
        for e in engs:
            self.eng[e]._auto_fence = True
        try:
            yield
        finally:
            for e in engs:
                self.eng[e]._auto_fence = saved[e]

    def done(self, e, instr=None):
        return self.mark(e)

    def after(self, waiter, *prods):
        w = self.eng[waiter]
        for p in prods:
            if isinstance(p, tuple):
                self.tick[p[0]].wait(w, p[1])
            else:
                self.tick[p].wait(w)

    def barrier(self, *engs):
        engs = engs or ENGS
        for e in engs:
            if e == "y":
                continue
            self.mark(e)
        for e in engs:
            if e == "y":
                continue
            self.after(e, *[p for p in engs if p != e and p != "y"])

    @staticmethod
    def _canon(n):
        parts = n.split("_", 1)
        if parts[0] in ("a0", "a1", "a2", "a3", "l2", "l3", "sf", "sp", "ln") and len(parts) > 1:
            return parts[1]
        return n

    def load(self, out, in_, q="y", sem=None):
        sname = self._canon(sem or ("dma_" + q))
        if sname not in self.dsem:
            self.dsem[sname] = Sem(self, sname)
        self.dsem[sname].inc(self.eng[q].dma_start(out=out, in_=in_), 16)

    def dwait(self, sem, waiter, val=None):
        sem = self._canon(sem)
        if sem in self.dsem:
            self.dsem[sem].wait(self.eng[waiter], val)

    def dma_wait(self, waiter, q="y", val=None):
        self.dwait("dma_" + q, waiter, val)

    def bank(self):
        b = self.banks[self.rrb % 8]
        self.rrb += 1
        return b

    def bind(self, name, existing):
        self._alias[name] = existing

    def arena(self, words=47104):
        return self.sbuf("ARENA", [128, words])

    def abuf(self, off, width):
        a = self.arena()
        return a[:, off:off + width]

    def sbuf(self, name, shape, dtype=F32):
        name = self._alias.get(name, name)
        if name not in self._sbufs:
            self._sbufs[name] = self.stack.enter_context(
                self.nc.sbuf_tensor(name, list(shape), dtype))
            self._sb_bytes += shape[0] * shape[1] * (2 if dtype == BF16 else 4)
        return self._sbufs[name]



def emit_load_X(ctx, X, x_hbm):
    xr = x_hbm.rearrange("(tc p) d -> p (tc d)", p=128)
    ctx.load(X[:, 0:NTC * D], xr, q="y")


def emit_store_X(ctx, X, y_hbm):
    yr = y_hbm.rearrange("(tc p) d -> p (tc d)", p=128)
    ctx.load(yr, X[:, 0:NTC * D], q="g")


def emit_tm_to_fm(ctx, XF, ident, blocks):
    for i0 in range(0, len(blocks), 4):
        grp = blocks[i0:i0 + 4]
        b = ctx.bank()
        b.begin(ctx.t)
        for k, (src, _) in enumerate(grp):
            ins = ctx.t.transpose(b.ap[:, k * 128:(k + 1) * 128], src, ident[:, :])
        b.fin_write(ins)
        b.begin_read(ctx.v)
        cols = [c for (_, c) in grp]
        if all(cols[j] + 128 == cols[j + 1] for j in range(len(cols) - 1)):
            ins = ctx.v.tensor_copy(XF[:, cols[0]:cols[0] + 128 * len(grp)],
                                    b.ap[:, 0:128 * len(grp)])
            b.fin_read(ins)
        else:
            for k, c in enumerate(cols):
                ins = ctx.v.tensor_copy(XF[:, c:c + 128], b.ap[:, k * 128:(k + 1) * 128])
            b.fin_read(ins)


def emit_stream_to_fm(ctx, X, XF, ident):
    blocks = []
    for dc in range(NDC):
        for tc in range(NTC):
            blocks.append((X[:, tc * D + dc * 128: tc * D + (dc + 1) * 128],
                           dc * TLOC + tc * 128))
    emit_tm_to_fm(ctx, XF, ident, blocks)


def emit_consts(ctx):
    ones1 = ctx.sbuf("c_ones1", [1, 128])
    ctx.v.memset(ones1[0:1, :], 1.0)
    onescol = ctx.sbuf("c_onescol", [128, 1])
    ctx.v.memset(onescol[:, 0:1], 1.0)
    ctx.mark("v")
    ctx.after("t", "v")


def emit_bcast_col(ctx, src_col, dst, W, ident):
    t, v = ctx.t, ctx.v
    ones1 = ctx.sbuf("c_ones1", [1, 128])
    ctx.mark("v")
    ctx.after("t", "v")
    b = ctx.bank()
    b.begin(t)
    ins = t.transpose(b.ap[0:1, 0:W], src_col, ident[0:W, 0:W])
    b.fin_write(ins)
    b.begin_read(v)
    row = ctx.sbuf("c_row", [1, 512])
    ins = v.tensor_copy(row[0:1, 0:W], b.ap[0:1, 0:W])
    b.fin_read(ins)
    ctx.mark("v")
    ctx.after("t", "v")
    b2 = ctx.bank()
    b2.begin(t)
    ins = t.matmul(b2.ap[:, 0:W], ones1[0:1, :], row[0:1, 0:W], start=True, stop=True)
    b2.fin_write(ins)
    b2.begin_read(v)
    ins = v.tensor_copy(dst, b2.ap[:, 0:W])
    b2.fin_read(ins)


def emit_rsqrt(ctx, out, a_col, eps, tmp):
    v, s = ctx.v, ctx.s
    with ctx.fenced("v", "s"):
        v.tensor_scalar(tmp[:, 2:3], a_col, eps, None, ALU.add)
        ctx.after("s", "v")
        s.activation(tmp[:, 0:1], tmp[:, 2:3], AF.Sqrt)
        ctx.after("v", "s")
        v.reciprocal(tmp[:, 1:2], tmp[:, 0:1])
        v.tensor_tensor(tmp[:, 3:4], tmp[:, 1:2], tmp[:, 1:2], ALU.mult)
        v.tensor_tensor(tmp[:, 3:4], tmp[:, 3:4], tmp[:, 2:3], ALU.mult)
        v.tensor_scalar(tmp[:, 3:4], tmp[:, 3:4], -0.5, 1.5, ALU.mult, ALU.add)
        return v.tensor_tensor(out, tmp[:, 1:2], tmp[:, 3:4], ALU.mult)


def emit_residual_ln(ctx, X, F_, tc, eps=1e-5):
    v, s = ctx.v, ctx.s
    r = ctx.sbuf("ln_r", [128, D])
    st = ctx.sbuf("ln_st", [128, 8])
    v.scalar_tensor_tensor(r[:, :], F_, 1.0, X[:, tc * D:(tc + 1) * D],
                           ALU.mult, ALU.add, accum_out=st[:, 0:1])
    ctx.mark("v")
    ctx.after("s", "v")
    sq = ctx.abuf(ctx.ln_off[2], D)
    s.activation(sq, r, AF.Square, accum_out=st[:, 1:2])
    ctx.fence("s")
    ctx.mark("s")
    ctx.after("v", "s")
    inv_n = 1.0 / D
    with ctx.fenced("v"):
        v.tensor_scalar(st[:, 2:4], st[:, 0:2], inv_n, None, ALU.mult)
        v.tensor_tensor(st[:, 4:5], st[:, 2:3], st[:, 2:3], ALU.mult)
        v.tensor_tensor(st[:, 4:5], st[:, 3:4], st[:, 4:5], ALU.subtract)
    ctx.after("s", "v")
    tmp = ctx.sbuf("ln_tmp", [128, 4])
    emit_rsqrt(ctx, st[:, 5:6], st[:, 4:5], eps, tmp)
    with ctx.fenced("v"):
        v.tensor_tensor(st[:, 6:7], st[:, 2:3], st[:, 5:6], ALU.mult)
        v.tensor_scalar(st[:, 6:7], st[:, 6:7], -1.0, None, ALU.mult)
    ctx.after("s", "v")
    ins = s.activation(X[:, tc * D:(tc + 1) * D], r[:, :], AF.Identity,
                       bias=st[:, 6:7], scale=st[:, 5:6])
    ctx.done("s", ins)
    ctx.after("v", "s")



def emit_fp8_round_cols(ctx, out, vin, w, t1, t2, t3):
    v = ctx.v
    K = float(2 ** 20 + 1)
    C = float(1.5 * 2 ** 14)
    v.tensor_scalar(t1, vin, K, None, ALU.mult)
    v.tensor_tensor(t2, t1, vin, ALU.subtract)
    v.tensor_tensor(t1, t1, t2, ALU.subtract)
    v.tensor_scalar(t2, vin, C, C, ALU.add, ALU.subtract)
    v.tensor_scalar(t3, vin, -1.0, None, ALU.mult)
    v.tensor_tensor(t3, t3, vin, ALU.max)
    v.tensor_scalar(t3, t3, float(2 ** -6), None, ALU.is_lt)
    v.tensor_tensor(t2, t2, t1, ALU.subtract)
    v.tensor_tensor(t2, t2, t3, ALU.mult)
    return v.tensor_tensor(out, t1, t2, ALU.add)


def emit_ln_writeback(ctx, xs_tile, F_, xdst, tc, eps=1e-5):
    v, s = ctx.v, ctx.s
    r = ctx.abuf(ctx.ln_off[0], D)
    st = ctx.sbuf("ln_st", [128, 8])
    out = ctx.abuf(ctx.ln_off[1], D)
    ctx.after("v", "s")
    ctx.dwait("ln_xdst", "s")
    v.scalar_tensor_tensor(r, F_, 1.0, xs_tile,
                           ALU.mult, ALU.add, accum_out=st[:, 0:1])
    ctx.fence("v")
    ctx.mark("v")
    ctx.after("s", "v")
    sq = ctx.abuf(ctx.ln_off[2], D)
    s.activation(sq, r, AF.Square, accum_out=st[:, 1:2])
    ctx.fence("s")
    ctx.mark("s")
    ctx.after("v", "s")
    inv_n = 1.0 / D
    with ctx.fenced("v"):
        v.tensor_scalar(st[:, 2:4], st[:, 0:2], inv_n, None, ALU.mult)
        v.tensor_tensor(st[:, 4:5], st[:, 2:3], st[:, 2:3], ALU.mult)
        v.tensor_tensor(st[:, 4:5], st[:, 3:4], st[:, 4:5], ALU.subtract)
    ctx.after("s", "v")
    tmp = ctx.sbuf("ln_tmp", [128, 4])
    emit_rsqrt(ctx, st[:, 5:6], st[:, 4:5], eps, tmp)
    with ctx.fenced("v"):
        v.tensor_tensor(st[:, 6:7], st[:, 2:3], st[:, 5:6], ALU.mult)
        v.tensor_scalar(st[:, 6:7], st[:, 6:7], -1.0, None, ALU.mult)
    ctx.after("s", "v")
    ins = s.activation(out, r, AF.Identity, bias=st[:, 6:7], scale=st[:, 5:6])
    ctx.done("s", ins)
    ctx.after("g", "s")
    ctx.load(xdst[tc * 128:(tc + 1) * 128, :], out, q="g", sem="ln_xdst")


def load_x_chunk(ctx, dst_tile, xsrc, tc, q="y"):
    ctx.load(dst_tile, xsrc[tc * 128:(tc + 1) * 128, :], q=q)


def emit_l3ffn(ctx, xsrc, xdst, w_hbm, wq_dram, ident):
    v, s, t, g = ctx.v, ctx.s, ctx.t, ctx.g
    ctx.ln_off = (36864, 34816, 32768)
    WB = ctx.abuf(0, NDC * 512)
    WQS = ctx.abuf(8192, NDC * 512)
    wst = ctx.sbuf("l3_wst", [128, 64])
    bamax = ctx.sbuf("l3_bamax", [128, 64])
    sbc = ctx.sbuf("l3_sbc", [128, 64])
    ibc = ctx.sbuf("l3_ibc", [128, 64])
    tw1 = ctx.abuf(32768, D)
    tw2 = ctx.abuf(34816, D)
    tw3 = ctx.abuf(36864, D)
    for ot in range(4):
        if ot:
            ctx.after("y", "v")
        ctx.load(WB, w_hbm[ot], q="y", sem="l3_wb")
        ctx.dwait("l3_wb", "v")
        for dc in range(NDC):
            v.tensor_reduce(wst[:, dc * 4:(dc + 1) * 4],
                            WB[:, dc * 512:(dc + 1) * 512].rearrange("p (b j) -> p b j", j=128),
                            AX.X, ALU.max, apply_absolute_value=True)
        ctx.mark("v")
        ctx.after("t", "v")
        bb = ctx.bank()
        bb.begin(t)
        ins = t.transpose(bb.ap[0:64, 0:128], wst[:, 0:64], ident[:, :])
        bb.fin_write(ins)
        bb.begin_read(v)
        ins = v.tensor_reduce(bamax[0:64, 0:1], bb.ap[0:64, 0:128], AX.X, ALU.max)
        ctx.fence("v")
        bb.fin_read(ins)
        with ctx.fenced("v"):
            v.tensor_scalar(bamax[0:64, 0:1], bamax[0:64, 0:1], 1e-4, None, ALU.max)
            v.reciprocal(bamax[0:64, 1:2], bamax[0:64, 0:1])
            v.tensor_scalar(bamax[0:64, 1:2], bamax[0:64, 1:2], 448.0, None, ALU.mult)
            v.reciprocal(bamax[0:64, 2:3], bamax[0:64, 1:2])
        emit_bcast_col(ctx, bamax[0:64, 1:2], sbc[:, 0:64], 64, ident)
        emit_bcast_col(ctx, bamax[0:64, 2:3], ibc[:, 0:64], 64, ident)
        if ot:
            ctx.dwait("l3_wqst", "v")
        for dc in range(NDC):
            for jb in range(4):
                blk = dc * 4 + jb
                cs = slice(dc * 512 + jb * 128, dc * 512 + (jb + 1) * 128)
                v.tensor_scalar(WQS[:, cs], WB[:, cs], sbc[:, blk:blk + 1], None, ALU.mult)
            emit_fp8_round_cols(ctx, WQS[:, dc * 512:(dc + 1) * 512],
                                WQS[:, dc * 512:(dc + 1) * 512], 512,
                                tw1[:, 0:512], tw2[:, 0:512], tw3[:, 0:512])
            for jb in range(4):
                blk = dc * 4 + jb
                cs = slice(dc * 512 + jb * 128, dc * 512 + (jb + 1) * 128)
                v.tensor_scalar(WQS[:, cs], WQS[:, cs], ibc[:, blk:blk + 1], None, ALU.mult)
        ctx.mark("v")
        ctx.after("g", "v")
        ctx.load(wq_dram[ot], WQS, q="g", sem="l3_wqst")

    XS = ctx.abuf(16384, 4 * D)
    XQ = ctx.abuf(38912, D)
    XF = ctx.abuf(24576, NDC * 512)
    FO = ctx.sbuf("bf16_fo", [128, 4 * D], BF16)
    FO32 = ctx.abuf(38912, D)
    WQB = [ctx.abuf(0, NDC * 512), ctx.abuf(8192, NDC * 512)]
    xst = ctx.sbuf("l3_xst", [128, 48])
    t1 = ctx.abuf(32768, D)
    t2 = ctx.abuf(34816, D)
    t3 = ctx.abuf(36864, D)
    ctx.dwait("l3_wqst", "t")
    ctx.dwait("l3_wqst", "y")
    ctx.after("y", "v")
    for tg in range(4):
        if tg:
            ctx.after("y", "v")
        ctx.after("v", "s")
        ctx.dwait("ln_xdst", "v")
        for k in range(4):
            ctx.load(XS[:, k * D:(k + 1) * D],
                     xsrc[(tg * 4 + k) * 128:(tg * 4 + k + 1) * 128, :], q="y", sem="l3_xs")
        ctx.dwait("l3_xs", "v")
        for k in range(4):
            xtile = XS[:, k * D:(k + 1) * D]
            v.tensor_reduce(xst[:, 0:16], xtile.rearrange("p (b j) -> p b j", j=128),
                            AX.X, ALU.max, apply_absolute_value=True)
            ctx.fence("v")
            with ctx.fenced("v"):
                v.tensor_scalar(xst[:, 0:16], xst[:, 0:16], 1e-4, None, ALU.max)
                v.reciprocal(xst[:, 16:32], xst[:, 0:16])
                v.tensor_scalar(xst[:, 16:32], xst[:, 16:32], 448.0, None, ALU.mult)
                v.reciprocal(xst[:, 32:48], xst[:, 16:32])
            if tg or k:
                ctx.t_gate("v")
            for blk in range(16):
                cs = slice(blk * 128, (blk + 1) * 128)
                v.tensor_scalar(XQ[:, cs], xtile[:, cs], xst[:, 16 + blk:17 + blk], None, ALU.mult)
            emit_fp8_round_cols(ctx, XQ, XQ, D, t1, t2, t3)
            for blk in range(16):
                cs = slice(blk * 128, (blk + 1) * 128)
                v.tensor_scalar(XQ[:, cs], XQ[:, cs], xst[:, 32 + blk:33 + blk], None, ALU.mult)
            ctx.mark("v")
            ctx.after("t", "v")
            emit_tm_to_fm(ctx, XF, ident,
                          [(XQ[:, dc * 128:(dc + 1) * 128], dc * 512 + k * 128)
                           for dc in range(NDC)])
        ctx.mark("v")
        ctx.after("t", "v")
        for ot in range(4):
            wb = WQB[ot % 2]
            if ot >= 2 or tg:
                ctx.t_gate("y")
            ctx.load(wb, wq_dram[ot], q="y", sem=f"l3_wqb{ot % 2}")
            ctx.dwait(f"l3_wqb{ot % 2}", "t")
            for k in range(4):
                b = ctx.bank()
                b.begin(t)
                for dc in range(NDC):
                    ins = t.matmul(b.ap[:, 0:512],
                                   XF[:, dc * 512 + k * 128: dc * 512 + (k + 1) * 128],
                                   wb[:, dc * 512:(dc + 1) * 512],
                                   start=(dc == 0), stop=(dc == NDC - 1))
                b.fin_write(ins)
                b.begin_read(s)
                ins = s.copy(FO[:, k * D + ot * 512: k * D + (ot + 1) * 512], b.ap[:, 0:512])
                b.fin_read(ins)
        ctx.mark("s")
        ctx.after("v", "s")
        for k in range(4):
            v.tensor_copy(FO32, FO[:, k * D:(k + 1) * D])
            emit_ln_writeback(ctx, XS[:, k * D:(k + 1) * D], FO32, xdst, tg * 4 + k)
    ctx.barrier("t", "v", "s", "g")



def stage_w_strips(w, OS=512):
    O, H = w.shape
    ndc = H // 128
    n = O // OS
    wt = w.reshape(n, OS, ndc, 128)
    return np.ascontiguousarray(wt.transpose(0, 3, 2, 1)).reshape(n, 128, ndc * OS)





INTER2 = 5632
NIC2 = INTER2 // 128

def emit_l2ffn(ctx, xsrc, xdst, w1_hbm, w2_hbm, w3_hbm, q1_dram, q2_dram, q3_dram, ident):
    v, s, t, g = ctx.v, ctx.s, ctx.t, ctx.g
    XF = ctx.abuf(0, 8192)
    FG = ctx.abuf(0, 8192)
    IG = ctx.abuf(8192, NIC2 * 512)
    QB = [ctx.abuf(30720, 4096), ctx.abuf(34816, 4096)]
    Q3B = [ctx.abuf(38912, 512), ctx.abuf(39424, 512)]
    XC = ctx.abuf(39936, 2048)
    SGT = ctx.abuf(46080, 512)
    ctx.ln_off = (41984, 44032, 30720)
    acc = ctx.sbuf("l2_acc", [128, 8])
    gam = ctx.sbuf("l2_gam", [128, 8])

    v.memset(acc[:, 0:3], 0.0)
    srcs = [w1_hbm, w2_hbm, w3_hbm]
    WA = [ctx.abuf(30720, 2048), ctx.abuf(34816, 2048)]
    first = True
    for wi, wh in enumerate(srcs):
        for ic in range(NIC2):
            buf = WA[ic % 2]
            if not first and (ic >= 2 or wi):
                pass
            if not (wi == 0 and ic < 2):
                ctx.after("y", "v")
            ctx.load(buf, wh[ic], q="y", sem=f"l2_ga{ic % 2}")
            ctx.dwait(f"l2_ga{ic % 2}", "v")
            sub = ctx.sbuf("l2_sub", [128, 16])
            v.tensor_reduce(sub[:, 0:16], buf.rearrange("p (a b) -> p a b", b=128),
                            AX.X, ALU.add, apply_absolute_value=True)
            v.tensor_reduce(acc[:, 3:4], sub[:, 0:16], AX.X, ALU.add)
            v.tensor_tensor(acc[:, wi:wi + 1], acc[:, wi:wi + 1], acc[:, 3:4], ALU.add)
            first = False
    bb = ctx.bank()
    bb.begin(t)
    ctx.after("t", "v")
    ins = t.transpose(bb.ap[0:3, 0:128], acc[:, 0:3], ident[:, :])
    bb.fin_write(ins)
    bb.begin_read(v)
    ins = v.tensor_reduce(acc[0:3, 4:5], bb.ap[0:3, 0:128], AX.X, ALU.add)
    bb.fin_read(ins)
    NW = float(INTER2 * D)
    v.tensor_scalar(acc[0:3, 5:6], acc[0:3, 4:5], 1.0 / NW, None, ALU.mult)
    v.tensor_scalar(acc[0:3, 5:6], acc[0:3, 5:6], 1e-8, None, ALU.add)
    v.tensor_scalar(acc[0:3, 6:7], acc[0:3, 5:6], 0.5, None, ALU.mult)
    v.tensor_scalar(acc[0:3, 7:8], acc[0:3, 6:7], -1.0, None, ALU.mult)
    emit_bcast_col(ctx, acc[0:3, 6:7], gam[:, 0:3], 3, ident)
    emit_bcast_col(ctx, acc[0:3, 7:8], gam[:, 3:6], 3, ident)

    QS = ctx.abuf(8192, 2048)
    TQ = ctx.abuf(10240, 2048)
    qds = [q1_dram, q2_dram, q3_dram]
    for wi, (wh, qd) in enumerate(zip(srcs, qds)):
        for ic in range(NIC2):
            buf = WA[ic % 2]
            ctx.after("y", "v")
            ctx.load(buf, wh[ic], q="y", sem=f"l2_ga{ic % 2}")
            ctx.dwait(f"l2_ga{ic % 2}", "v")
            ctx.dwait("l2_qst", "v")
            v.tensor_scalar(QS, buf, gam[:, wi:wi + 1], None, ALU.is_gt)
            v.tensor_scalar(TQ, buf, gam[:, 3 + wi:4 + wi], None, ALU.is_lt)
            v.tensor_tensor(QS, QS, TQ, ALU.subtract)
            ctx.after("g", "v")
            ctx.load(qd[ic], QS, q="g", sem="l2_qst")
    ctx.dwait("l2_qst", "y")
    ctx.dwait("l2_qst", "t")

    for tg in range(4):
        ctx.after("y", "s")
        ctx.after("y", "v")
        ctx.t_gate("v")
        ctx.dwait("ln_xdst", "v")
        ctx.dwait("ln_xdst", "y")
        for k in range(4):
            if tg or k:
                ctx.t_gate("y")
            ctx.load(XC, xsrc[(tg * 4 + k) * 128:(tg * 4 + k + 1) * 128, :], q="y", sem="l2_xc")
            ctx.dwait("l2_xc", "t")
            emit_tm_to_fm(ctx, XF, ident,
                          [(XC[:, dc * 128:(dc + 1) * 128], dc * 512 + k * 128)
                           for dc in range(NDC)])
        ctx.after("t", "v")
        for ic in range(NIC2):
            qb = QB[ic % 2]
            if ic >= 2 or tg:
                ctx.t_gate("y")
            ctx.load(qb[:, 0:2048], q1_dram[ic], q="y", sem=f"l2_c1{ic % 2}")
            ctx.load(qb[:, 2048:4096], q2_dram[ic], q="y", sem=f"l2_c1{ic % 2}")
            ctx.dwait(f"l2_c1{ic % 2}", "t")
            bA = ctx.banks[4 + 2 * (ic % 2)]
            bB = ctx.banks[5 + 2 * (ic % 2)]
            bA.begin(t)
            bB.begin(t)
            for hc in range(NDC):
                i1 = t.matmul(bA.ap[:, 0:512], qb[:, hc * 128:(hc + 1) * 128],
                              XF[:, hc * 512:(hc + 1) * 512],
                              start=(hc == 0), stop=(hc == NDC - 1))
            bA.fin_write(i1)
            for hc in range(NDC):
                i2 = t.matmul(bB.ap[:, 0:512], qb[:, 2048 + hc * 128: 2048 + (hc + 1) * 128],
                              XF[:, hc * 512:(hc + 1) * 512],
                              start=(hc == 0), stop=(hc == NDC - 1))
            bB.fin_write(i2)
            bA.begin_read("s")
            ins = s.activation(SGT, bA.ap[:, 0:512], AF.Sigmoid)
            bA.fin_read(ins)
            ctx.after("v", "s")
            bB.begin_read("v")
            ins = v.tensor_tensor(IG[:, ic * 512:(ic + 1) * 512], SGT, bB.ap[:, 0:512], ALU.mult)
            bB.fin_read(ins)
        ctx.after("t", "v")
        csem = ctx.gsem("l2_c2done")
        for ostrip in range(4):
            obanks = [ctx.banks[i] for i in range(4)]
            for b in obanks:
                b.begin(t)
            for ic in range(NIC2):
                q3b = Q3B[ic % 2]
                if ic >= 2:
                    csem.wait(ctx.y, csem.n - 1)
                elif ostrip or tg:
                    csem.wait(ctx.y)
                    ctx.t_gate("y")
                ctx.load(q3b, q3_dram[ic][:, ostrip * 512:(ostrip + 1) * 512],
                         q="y", sem=f"l2_c2{ic % 2}")
                ctx.dwait(f"l2_c2{ic % 2}", "t")
                for st in range(4):
                    ins = t.matmul(obanks[st].ap[:, 0:512],
                                   IG[:, ic * 512 + st * 128: ic * 512 + (st + 1) * 128],
                                   q3b,
                                   start=(ic == 0), stop=(ic == NIC2 - 1))
                    if ic == NIC2 - 1:
                        obanks[st].fin_write(ins)
                if ic < NIC2 - 1:
                    csem.inc(ins)
            for st in range(4):
                obanks[st].begin_read("v")
                ins = v.tensor_copy(FG[:, st * 2048 + ostrip * 512: st * 2048 + (ostrip + 1) * 512],
                                    obanks[st].ap[:, 0:512])
                obanks[st].fin_read(ins)
        for k in range(4):
            ctx.t_gate("y")
            ctx.after("y", "v")
            ctx.load(XC, xsrc[(tg * 4 + k) * 128:(tg * 4 + k + 1) * 128, :], q="y", sem="l2_xc")
            ctx.dwait("l2_xc", "v")
            emit_ln_writeback(ctx, XC, FG[:, k * 2048:(k + 1) * 2048], xdst, tg * 4 + k)
    ctx.barrier("t", "v", "s", "g")


def emit_block_entry(ctx):
    ctx.after("y", "v")
    ctx.after("y", "s")
    ctx.t_gate("y")
    ctx.dwait("ln_xdst", "y")
    ctx.barrier("t", "v", "s", "g")


def stage_l2w(w):
    O, H = w.shape
    wt = w.reshape(NIC2, 128, NDC, 128)
    return np.ascontiguousarray(wt.transpose(0, 3, 2, 1)).reshape(NIC2, 128, H)


def stage_l2w3(w3):
    O, I = w3.shape
    wt = w3.T.reshape(NIC2, 128, O)
    return np.ascontiguousarray(wt)



NCTRL = 130156
NSAMP = 16777216
NQ = NSAMP // 128
NSUP = NQ // 1024
NTAP = 7
INTERF = 8192
NICF = INTERF // 128

def build_spline_tables():
    n = NSAMP
    t = np.linspace(0.0, NCTRL - 1.0, n, dtype=np.float64)
    i = np.clip(np.floor(t).astype(np.int64), 0, NCTRL - 2)
    f = (t - i).astype(np.float32)
    f64 = f.astype(np.float64)
    f2, f3 = f64 * f64, f64 * f64 * f64
    cks = [0.5 * (-f3 + 2 * f2 - f64),
           0.5 * (3 * f3 - 5 * f2 + 2.0),
           0.5 * (-3 * f3 + 4 * f2 + f64),
           0.5 * (f3 - f2)]
    cks = [c.astype(np.float32) for c in cks]
    taps = np.zeros((NSUP, 128, NTAP, 1024), dtype=np.float32)
    j = np.arange(n, dtype=np.int64)
    q = j // 128
    r = (j % 128).astype(np.int64)
    st = q // 1024
    u = q % 1024
    tl = u // 128
    w0 = np.asarray(build_w0(), dtype=np.int64)
    for k in range(4):
        cpidx = np.clip(i + (k - 1), 0, NCTRL - 1)
        m = cpidx - w0[st] - (u - tl)
        assert m.min() >= 0 and m.max() < NTAP, (k, m.min(), m.max())
        np.add.at(taps, (st, r, m, u), cks[k])
    return taps.reshape(NSUP, 128, NTAP * 1024), [int(x) for x in w0]


def build_w0():
    n = NSAMP
    t = np.linspace(0.0, NCTRL - 1.0, n, dtype=np.float64)
    i = np.clip(np.floor(t).astype(np.int64), 0, NCTRL - 2)
    j = np.arange(n, dtype=np.int64)
    q = j // 128
    st = q // 1024
    u = q % 1024
    tl = u // 128
    base = np.clip(i - 1, 0, NCTRL - 1) - (u - tl)
    w0 = np.full(NSUP, 10**9, np.int64)
    np.minimum.at(w0, st, base)
    return [int(x) for x in w0]


CPB_W = 1040

def emit_spline_recon(ctx, cps_hbm, taps_hbm, w0, s_drams, ident):
    v, s, t, g = ctx.v, ctx.s, ctx.t, ctx.g
    TB = [ctx.abuf(0, NTAP * 1024), ctx.abuf(7168, NTAP * 1024)]
    CPB = ctx.abuf(14336, CPB_W)
    SO = [ctx.abuf(15376, 1024), ctx.abuf(16400, 1024)]
    TMP = ctx.abuf(17424, 1024)
    CROW = ctx.abuf(18448, CPB_W)
    ones1 = ctx.sbuf("c_ones1", [1, 128])
    so_i = 0
    nmat = len(cps_hbm)
    for sti in range(NSUP):
        tb = TB[sti % 2]
        if sti >= 2:
            ctx.after("y", "v")
        ctx.load(tb, taps_hbm[sti], q="y", sem=f"sp_tb{sti % 2}")
        ctx.dwait(f"sp_tb{sti % 2}", "v")
        for mat in range(nmat):
            ctx.after("y", "v")
            ctx.load(CROW[0:1, 0:CPB_W], cps_hbm[mat][w0[sti] + 2:w0[sti] + 2 + CPB_W],
                     q="y", sem="sp_cp")
            ctx.dwait("sp_cp", "t")
            for seg in range(0, CPB_W, 512):
                wseg = min(512, CPB_W - seg)
                b = ctx.bank()
                b.begin(t)
                ins = t.matmul(b.ap[:, 0:wseg], ones1[0:1, :],
                               CROW[0:1, seg:seg + wseg], start=True, stop=True)
                b.fin_write(ins)
                b.begin_read(v)
                ins = v.tensor_copy(CPB[:, seg:seg + wseg], b.ap[:, 0:wseg])
                b.fin_read(ins)
            par = so_i % 2
            so = SO[par]
            if so_i >= 2:
                ctx.dwait(f"sp_so{par}", "v")
            so_i += 1
            first = True
            for m in range(NTAP):
                rd = spline_cpb_ap(CPB, m)
                if first:
                    v.tensor_tensor(so, tb[:, m * 1024:(m + 1) * 1024], rd, ALU.mult)
                    first = False
                else:
                    v.tensor_tensor(TMP, tb[:, m * 1024:(m + 1) * 1024], rd, ALU.mult)
                    v.tensor_tensor(so, so, TMP, ALU.add)
            ctx.after("g", "v")
            ctx.load(s_drams[mat][:, sti * 1024:(sti + 1) * 1024], so,
                     q="g", sem=f"sp_so{par}")
    ctx.dwait("sp_so0", "y")
    ctx.dwait("sp_so1", "y")
    ctx.dwait("sp_so0", "t")
    ctx.dwait("sp_so1", "t")
    ctx.barrier("t", "v", "s", "g")


def spline_cpb_ap(CPB, m):
    pitch = CPB.tensor.shape[1]
    return bass.AP(CPB.tensor, CPB.offset + m,
                   [[pitch, 128], [127, 8], [1, 128]])


def emit_spline_ffn(ctx, xsrc, xdst, sg_dram, su_dram, sd_dram, ident):
    v, s, t, g = ctx.v, ctx.s, ctx.t, ctx.g
    XF = ctx.abuf(0, 4096)
    IG = ctx.abuf(4096, 64 * 256)
    FG = ctx.abuf(20480, 4096)
    QB = [ctx.abuf(24576, 4096), ctx.abuf(28672, 4096)]
    SEG = ctx.abuf(32768, 8192)
    XC = ctx.abuf(40960, 2048)
    ctx.ln_off = (43008, 45056, 24576)
    SGT = ctx.abuf(40960, 256)
    for tg in range(8):
        ctx.after("y", "s")
        ctx.after("y", "v")
        ctx.t_gate("v")
        ctx.dwait("ln_xdst", "v")
        ctx.dwait("ln_xdst", "y")
        for k in range(2):
            if tg or k:
                ctx.t_gate("y")
                ctx.after("y", "v")
            ctx.load(XC, xsrc[(tg * 2 + k) * 128:(tg * 2 + k + 1) * 128, :], q="y", sem="sf_xc")
            ctx.dwait("sf_xc", "t")
            emit_tm_to_fm(ctx, XF, ident,
                          [(XC[:, dc * 128:(dc + 1) * 128], dc * 256 + k * 128)
                           for dc in range(NDC)])
        ctx.after("t", "v")
        if not hasattr(ctx, "_sf_qbdone"):
            ctx._sf_qbdone = {0: [], 1: []}
        for ic in range(NICF):
            qb = QB[ic % 2]
            for (sm, val) in ctx._sf_qbdone[ic % 2]:
                sm.wait(ctx.y, val)
            ctx.load(qb[:, 0:2048], sg_dram[:, ic * 2048:(ic + 1) * 2048], q="y", sem=f"sf_c1{ic % 2}")
            ctx.load(qb[:, 2048:4096], su_dram[:, ic * 2048:(ic + 1) * 2048], q="y", sem=f"sf_c1{ic % 2}")
            ctx.dwait(f"sf_c1{ic % 2}", "t")
            bA = ctx.banks[4 + 2 * (ic % 2)]
            bB = ctx.banks[5 + 2 * (ic % 2)]
            bA.begin(t)
            bB.begin(t)
            g16 = qb[:, 0:2048].rearrange("p (j s) -> p j s", s=16)
            u16 = qb[:, 2048:4096].rearrange("p (j s) -> p j s", s=16)
            for hc in range(NDC):
                i1 = t.matmul(bA.ap[:, 0:256], g16[:, :, hc],
                              XF[:, hc * 256:(hc + 1) * 256],
                              start=(hc == 0), stop=(hc == NDC - 1))
            bA.fin_write(i1)
            for hc in range(NDC):
                i2 = t.matmul(bB.ap[:, 0:256], u16[:, :, hc],
                              XF[:, hc * 256:(hc + 1) * 256],
                              start=(hc == 0), stop=(hc == NDC - 1))
            bB.fin_write(i2)
            ctx._sf_qbdone[ic % 2] = [(bA.sw, bA.sw.n), (bB.sw, bB.sw.n)]
            bA.begin_read("s")
            ins = s.activation(SGT, bA.ap[:, 0:256], AF.Silu)
            bA.fin_read(ins)
            ctx.after("v", "s")
            bB.begin_read("v")
            ins = v.tensor_tensor(IG[:, ic * 256:(ic + 1) * 256], SGT, bB.ap[:, 0:256], ALU.mult)
            bB.fin_read(ins)
        ctx.after("t", "v")
        for ostrip in range(16):
            if ostrip or tg:
                ctx.t_gate("y")
            ctx.load(SEG, sd_dram[:, ostrip * 8192:(ostrip + 1) * 8192], q="y", sem="sf_seg")
            ctx.dwait("sf_seg", "t")
            s64 = SEG.rearrange("p (o ci) -> p o ci", ci=64)
            for stt in range(2):
                b = ctx.banks[stt]
                b.begin(t)
                for ci in range(64):
                    ins = t.matmul(b.ap[:, 0:128],
                                   IG[:, ci * 256 + stt * 128: ci * 256 + (stt + 1) * 128],
                                   s64[:, :, ci],
                                   start=(ci == 0), stop=(ci == 63))
                b.fin_write(ins)
                b.begin_read("v")
                ins = v.tensor_copy(FG[:, stt * 2048 + ostrip * 128: stt * 2048 + (ostrip + 1) * 128],
                                    b.ap[:, 0:128])
                b.fin_read(ins)
        for k in range(2):
            ctx.t_gate("y")
            ctx.after("y", "v")
            ctx.load(XC, xsrc[(tg * 2 + k) * 128:(tg * 2 + k + 1) * 128, :], q="y", sem="sf_xc")
            ctx.dwait("sf_xc", "v")
            emit_ln_writeback(ctx, XC, FG[:, k * 2048:(k + 1) * 2048], xdst, tg * 2 + k)
    ctx.barrier("t", "v", "s", "g")


NCC3 = 32

def emit_l3attn(ctx, xsrc, xdst, win_hbm, wgate_hbm, wout_hbm, cw_hbm, cb_hbm,
                dtw_hbm, dtb_hbm, alog_hbm, bp_hbm, cp3_hbm, dp_hbm,
                proj_scr, st_scr, cum_scr, xt_scr, yg_scr, f_scr,
                ag1_src, ag1_dst, ag2_src, ag2_dst, ident, isodd, e2m):
    v, s, t, g = ctx.v, ctx.s, ctx.t, ctx.g
    rg = [[0, 1], [2, 3], [4, 5], [6, 7]]
    XF = ctx.abuf(0, 32768)
    PR = ctx.abuf(32768, 2051)
    DK = ctx.abuf(34819, 2048)
    DR = ctx.abuf(36867, 2048)
    SS = ctx.abuf(38915, 2048)
    TT = ctx.abuf(40963, 2048)
    ZT = ctx.abuf(43011, 2048)
    DT2 = ctx.sbuf("l3a_dt", [128, 64])
    CWB = ctx.sbuf("l3a_cw", [128, 16])
    E2 = ctx.sbuf("l3a_e2", [2, 128])
    MSK = ctx.sbuf("c_mask", [128, 1])
    ctx.load(E2[0:2, :], e2m, q="y", sem="a3_msk")
    ctx.load(MSK[0:1, 0:1], isodd, q="y", sem="a3_msk")
    ctx.dwait("a3_msk", "v")
    ctx.dwait("a3_msk", "t")
    emit_bcast_col(ctx, MSK[0:1, 0:1], MSK[:, 0:1], 1, ident)
    ctx.mark("v")
    ctx.after("t", "v")

    XC = ctx.abuf(43011, 2048)
    for tc in range(NTC):
        if tc:
            ctx.t_gate("y")
        ctx.load(XC, xsrc[tc * 128:(tc + 1) * 128, :], q="y", sem="a3_xc")
        ctx.dwait("a3_xc", "t")
        emit_tm_to_fm(ctx, XF, ident,
                      [(XC[:, dc * 128:(dc + 1) * 128], dc * 2048 + tc * 128)
                       for dc in range(NDC)])
    ctx.after("t", "v")

    WB = [ctx.abuf(34819, 2048), ctx.abuf(36867, 2048)]
    qbdone = {0: [], 1: []}
    for cc in range(NCC3):
        wb = WB[cc % 2]
        for (sm, val) in qbdone[cc % 2]:
            sm.wait(ctx.y, val)
        ctx.load(wb[:, :], win_hbm[cc], q="y", sem=f"a3_w{cc % 2}")
        ctx.dwait(f"a3_w{cc % 2}", "t")
        ctx.dwait("a3_pst", "v")
        lastb = None
        for tt4 in range(4):
            b = ctx.bank()
            b.begin(t)
            for hc in range(NDC):
                ins = t.matmul(b.ap[:, 0:512], wb[:, hc * 128:(hc + 1) * 128],
                               XF[:, hc * 2048 + tt4 * 512:hc * 2048 + (tt4 + 1) * 512],
                               start=(hc == 0), stop=(hc == NDC - 1))
            b.fin_write(ins)
            lastb = b
            b.begin_read("v")
            ins = v.tensor_copy(PR[:, 3 + tt4 * 512: 3 + (tt4 + 1) * 512], b.ap[:, 0:512])
            b.fin_read(ins)
        qbdone[cc % 2] = [(lastb.sw, lastb.sw.n)]
        ctx.after("g", "v")
        ctx.load(proj_scr[:, cc * 2048:(cc + 1) * 2048], PR[:, 3:2051], q="g", sem="a3_pst")
        v.tensor_copy(DT2[:, cc % 16 * 3:(cc % 16) * 3 + 3] if False else TT[:, cc * 3:cc * 3 + 3],
                      PR[:, 2048:2051])
    ctx.after("g", "v")
    ctx.load(ag1_src[:, 0:96], TT[:, 0:96], q="g", sem="a3_ag1")
    ctx.dwait("a3_ag1", "g")
    cc1 = g.collective_compute("AllGather", ALU.bypass, ins=[ag1_src[:, 0:96]],
                               outs=[ag1_dst[:, 0:96]], replica_groups=rg)
    agsem = ctx.gsem("a3_agsem")
    agsem.inc(cc1)
    agsem.wait(ctx.y)
    TAILS = ctx.sbuf("l3a_tails", [128, 96])
    ctx.load(TAILS[:, :], ag1_dst[0:128, 0:96], q="y", sem="a3_tl")
    ctx.dwait("a3_tl", "v")
    ctx.dwait("a3_pst", "y")

    ctx.after("v", "g")
    ctx.t_gate("v")
    v.memset(ZT, 0.0)
    ctx.mark("v")
    SFIN = ctx.sbuf("l3a_sfin", [128, 32])
    for cc in range(NCC3):
        ctx.after("y", "v")
        ctx.after("y", "s")
        ctx.load(PR[:, 3:2051], proj_scr[:, cc * 2048:(cc + 1) * 2048], q="y", sem="a3_pr")
        ctx.load(CWB[:, 0:4], cw_hbm[cc], q="y", sem="a3_pr")
        ctx.load(CWB[:, 4:5], cb_hbm[cc], q="y", sem="a3_pr")
        ctx.load(CWB[:, 5:6], alog_hbm[cc], q="y", sem="a3_pr")
        ctx.load(CWB[:, 6:7], bp_hbm[cc], q="y", sem="a3_pr")
        ctx.load(DT2[:, 0:2], dtw_hbm[cc], q="y", sem="a3_pr")
        ctx.load(DT2[0:2, 2:3], dtb_hbm[cc], q="y", sem="a3_pr")
        ctx.dwait("a3_pr", "v")
        ctx.dwait("a3_pr", "t")
        v.tensor_scalar(PR[:, 0:3], TAILS[:, cc * 3:cc * 3 + 3], MSK[:, 0:1], None, ALU.mult)
        v.tensor_scalar(SS, PR[:, 0:2048], CWB[:, 0:1], None, ALU.mult)
        for k in range(1, 4):
            v.scalar_tensor_tensor(SS, PR[:, k:k + 2048], CWB[:, k:k + 1], SS, ALU.mult, ALU.add)
        v.tensor_scalar(SS, SS, CWB[:, 4:5], None, ALU.add)
        ctx.after("s", "v")
        s.activation(TT, SS, AF.Silu)
        ctx.after("v", "s")
        ctx.after("g", "s")
        ctx.load(xt_scr[:, cc * 2048:(cc + 1) * 2048], TT, q="g", sem="a3_xt")
        ctx.after("t", "v")
        _reg = ctx.abuf(45056, 2048)
        dtp = _reg[0:2, :]
        for tt4 in range(4):
            b = ctx.bank()
            b.begin(t)
            ins = t.matmul(b.ap[0:2, 0:512], DT2[:, 0:2],
                           TT[:, tt4 * 512:(tt4 + 1) * 512], start=True, stop=True)
            b.fin_write(ins)
            b.begin_read("v")
            ins = v.tensor_scalar(dtp[:, tt4 * 512:(tt4 + 1) * 512], b.ap[0:2, 0:512],
                                  DT2[0:2, 2:3], None, ALU.add)
            b.fin_read(ins)
        wa = ctx.abuf(32768, 2048)[0:2, :]
        wb_ = ctx.abuf(34819, 2048)[0:2, :]
        v.tensor_scalar(wa, dtp, -1.0, None, ALU.mult)
        v.tensor_tensor(wa, wa, dtp, ALU.max)
        ctx.after("s", "v")
        s.activation(wb_, wa, AF.Exp, scale=-1.0)
        ctx.after("v", "s")
        v.tensor_scalar(wb_, wb_, 1.0, None, ALU.add)
        ctx.after("s", "v")
        s.activation(wb_, wb_, AF.Ln)
        ctx.after("v", "s")
        v.tensor_scalar(wa, dtp, 0.0, None, ALU.max)
        v.tensor_tensor(dtp, wa, wb_, ALU.add)
        ctx.after("t", "v")
        for tt4 in range(4):
            b2 = ctx.bank()
            b2.begin(t)
            ins = t.matmul(b2.ap[:, 0:512], E2[0:2, :], dtp[:, tt4 * 512:(tt4 + 1) * 512],
                           start=True, stop=True)
            b2.fin_write(ins)
            b2.begin_read("v")
            ins2 = v.tensor_copy(DK[:, tt4 * 512:(tt4 + 1) * 512], b2.ap[:, 0:512])
            b2.fin_read(ins2)
        ctx.after("s", "v")
        s.activation(CWB[:, 7:8], CWB[:, 5:6], AF.Exp)
        ctx.after("v", "s")
        v.tensor_scalar(CWB[:, 7:8], CWB[:, 7:8], -1.0, None, ALU.mult)
        v.tensor_scalar(SS, DK, CWB[:, 7:8], None, ALU.mult)
        ctx.after("s", "v")
        s.activation(SS, SS, AF.Exp)
        ctx.after("v", "s")
        v.tensor_scalar(DR, DK, CWB[:, 6:7], None, ALU.mult)
        v.tensor_tensor(DR, DR, TT, ALU.mult)
        v.tensor_tensor_scan(DK, SS, DR, 0.0, ALU.mult, ALU.add)
        ctx.dwait("a3_xt", "v")
        v.tensor_tensor_scan(TT, SS, ZT, 1.0, ALU.mult, ALU.add)
        v.tensor_copy(SFIN[:, cc:cc + 1], DK[:, 2047:2048])
        ctx.after("g", "v")
        ctx.dwait("a3_st", "v")
        ctx.load(st_scr[:, cc * 2048:(cc + 1) * 2048], DK, q="g", sem="a3_st")
        ctx.load(cum_scr[:, cc * 2048:(cc + 1) * 2048], TT, q="g", sem="a3_st")
    ctx.after("g", "v")
    ctx.load(ag2_src[:, 0:32], SFIN[:, 0:32], q="g", sem="a3_ag2")
    ctx.dwait("a3_ag2", "g")
    cc2 = g.collective_compute("AllGather", ALU.bypass, ins=[ag2_src[:, 0:32]],
                               outs=[ag2_dst[:, 0:32]], replica_groups=rg)
    agsem.inc(cc2)
    agsem.wait(ctx.y)
    RS = ctx.sbuf("l3a_rs", [128, 32])
    ctx.load(RS[:, :], ag2_dst[0:128, 0:32], q="y", sem="a3_rs")
    ctx.dwait("a3_rs", "v")
    ctx.dwait("a3_st", "y")
    ctx.dwait("a3_xt", "y")

    WB3 = [ctx.abuf(43011, 2048), ctx.abuf(45056, 2048)]
    qbdone3 = {0: [], 1: []}
    v.tensor_scalar(RS[:, :], RS[:, :], MSK[:, 0:1], None, ALU.mult)
    ctx.after("y", "v")
    for cc in range(NCC3):
        wb = WB3[cc % 2]
        for (sm, val) in qbdone3[cc % 2]:
            sm.wait(ctx.y, val)
        ctx.load(wb[:, :], wgate_hbm[cc], q="y", sem=f"a3_w{cc % 2}")
        ctx.after("y", "v")
        ctx.load(PR[:, 0:2048], st_scr[:, cc * 2048:(cc + 1) * 2048], q="y", sem="a3_p3")
        ctx.load(DK, cum_scr[:, cc * 2048:(cc + 1) * 2048], q="y", sem="a3_p3")
        ctx.load(DR, xt_scr[:, cc * 2048:(cc + 1) * 2048], q="y", sem="a3_p3")
        ctx.load(CWB[:, 8:9], cp3_hbm[cc], q="y", sem="a3_p3")
        ctx.load(CWB[:, 9:10], dp_hbm[cc], q="y", sem="a3_p3")
        ctx.dwait("a3_p3", "v")
        ctx.dwait(f"a3_w{cc % 2}", "t")
        v.scalar_tensor_tensor(SS, DK, RS[:, cc:cc + 1], PR[:, 0:2048], ALU.mult, ALU.add)
        v.tensor_scalar(SS, SS, CWB[:, 8:9], None, ALU.mult)
        v.scalar_tensor_tensor(SS, DR, CWB[:, 9:10], SS, ALU.mult, ALU.add)
        lastb = None
        for tt4 in range(4):
            b = ctx.bank()
            b.begin(t)
            for hc in range(NDC):
                ins = t.matmul(b.ap[:, 0:512], wb[:, hc * 128:(hc + 1) * 128],
                               XF[:, hc * 2048 + tt4 * 512:hc * 2048 + (tt4 + 1) * 512],
                               start=(hc == 0), stop=(hc == NDC - 1))
            b.fin_write(ins)
            lastb = b
            b.begin_read("s")
            ins = s.activation(TT[:, tt4 * 512:(tt4 + 1) * 512], b.ap[:, 0:512], AF.Sigmoid)
            b.fin_read(ins)
        qbdone3[cc % 2] = [(lastb.sw, lastb.sw.n)]
        ctx.after("v", "s")
        v.tensor_tensor(SS, SS, TT, ALU.mult)
        ctx.after("g", "v")
        ctx.dwait("a3_yg", "v")
        ctx.load(yg_scr[:, cc * 2048:(cc + 1) * 2048], SS, q="g", sem="a3_yg")
    ctx.dwait("a3_yg", "y")

    WOS = ctx.abuf(0, 16384)
    YGS = ctx.abuf(16384, 4096)
    for ostrip in range(4):
        ctx.t_gate("y")
        for cc in range(NCC3):
            ctx.load(WOS[:, cc * 512:(cc + 1) * 512],
                     wout_hbm[cc][:, ostrip * 512:(ostrip + 1) * 512], q="y", sem="a3_wo")
        for tch in range(NTC):
            ctx.t_gate("y")
            for cc in range(NCC3):
                ctx.load(YGS[:, cc * 128:(cc + 1) * 128],
                         yg_scr[:, cc * 2048 + tch * 128: cc * 2048 + (tch + 1) * 128],
                         q="y", sem="a3_ygs")
            ctx.dwait("a3_wo", "t")
            ctx.dwait("a3_ygs", "t")
            b = ctx.bank()
            b.begin(t)
            for cc in range(NCC3):
                ins = t.matmul(b.ap[:, 0:512], YGS[:, cc * 128:(cc + 1) * 128],
                               WOS[:, cc * 512:(cc + 1) * 512],
                               start=(cc == 0), stop=(cc == NCC3 - 1))
            b.fin_write(ins)
            ctx.dwait("a3_f", "v")
            b.begin_read("v")
            ins = v.tensor_copy(SS[:, 0:512], b.ap[:, 0:512])
            b.fin_read(ins)
            ctx.after("g", "v")
            ctx.load(f_scr[:, tch * 2048 + ostrip * 512: tch * 2048 + (ostrip + 1) * 512],
                     SS[:, 0:512], q="g", sem="a3_f")
    ctx.dwait("a3_f", "y")

    ctx.ln_off = (34819, 36867, 43011)
    FC = ctx.abuf(40963, 2048)
    XC5 = ctx.abuf(45056, 2048)
    for tc in range(NTC):
        ctx.after("y", "v")
        ctx.after("y", "s")
        ctx.load(XC5, xsrc[tc * 128:(tc + 1) * 128, :], q="y", sem="a3_x5")
        ctx.load(FC, f_scr[:, tc * 2048:(tc + 1) * 2048], q="y", sem="a3_x5")
        ctx.dwait("a3_x5", "v")
        emit_ln_writeback(ctx, XC5, FC, xdst, tc)
    ctx.barrier("t", "v", "s", "g")



def emit_l1attn(ctx, xsrc, xdst, wqkv_hbm, bqkv_hbm, wproj_hbm, bproj_hbm,
                q_scr, att_scr, ag_src, ag_dst, ident, tbase_hbm, colidx_hbm):
    v, s, t, g = ctx.v, ctx.s, ctx.t, ctx.g
    rg = [[0, 1], [2, 3], [4, 5], [6, 7]]
    SCALE = float(128 ** -0.5)
    XFF = ctx.abuf(0, 32768)
    WB = [ctx.abuf(32768, 2048), ctx.abuf(34816, 2048)]
    ATW = ctx.abuf(36864, 2048)
    XC = ctx.abuf(38912, 2048)
    KVO = ctx.abuf(40960, 4096)
    TB = ctx.sbuf("l1_tb", [128, 8])
    BC = ctx.sbuf("l1_bc", [128, 2])
    ST = ctx.sbuf("l1_st", [128, 8])
    ctx.load(TB[:, 0:1], tbase_hbm, q="y", sem="a1_tb")
    ctx.dwait("a1_tb", "v")

    for tc in range(NTC):
        ctx.t_gate("y")
        ctx.after("y", "v")
        ctx.load(XC, xsrc[tc * 128:(tc + 1) * 128, :], q="y", sem="a1_xc")
        ctx.dwait("a1_xc", "t")
        emit_tm_to_fm(ctx, XFF, ident,
                      [(XC[:, dc * 128:(dc + 1) * 128], dc * 2048 + tc * 128)
                       for dc in range(NDC)])
    ctx.after("t", "v")

    qbdone = {0: [], 1: []}
    for oc in range(18):
        wb = WB[oc % 2]
        for (sm, val) in qbdone[oc % 2]:
            sm.wait(ctx.y, val)
        ctx.load(wb, wqkv_hbm[oc], q="y", sem=f"a1_w{oc % 2}")
        ctx.load(BC[:, 0:1], bqkv_hbm[oc], q="y", sem=f"a1_w{oc % 2}")
        ctx.dwait(f"a1_w{oc % 2}", "t")
        ctx.dwait(f"a1_w{oc % 2}", "s")
        if oc < 16:
            ctx.dwait("a1_qst", "s")
        lastb = None
        for tt4 in range(4):
            b = ctx.bank()
            b.begin(t)
            for hc in range(NDC):
                ins = t.matmul(b.ap[:, 0:512], wb[:, hc * 128:(hc + 1) * 128],
                               XFF[:, hc * 2048 + tt4 * 512: hc * 2048 + (tt4 + 1) * 512],
                               start=(hc == 0), stop=(hc == NDC - 1))
            b.fin_write(ins)
            lastb = b
            b.begin_read("s")
            if oc < 16:
                ins = s.activation(ATW[:, tt4 * 512:(tt4 + 1) * 512], b.ap[:, 0:512],
                                   AF.Identity, bias=BC[:, 0:1])
            else:
                ins = s.activation(
                    KVO[:, (oc - 16) * 2048 + tt4 * 512:(oc - 16) * 2048 + (tt4 + 1) * 512],
                    b.ap[:, 0:512], AF.Identity, bias=BC[:, 0:1])
            b.fin_read(ins)
        qbdone[oc % 2] = [(lastb.sw, lastb.sw.n)]
        if oc < 16:
            ctx.after("g", "s")
            ctx.load(q_scr[:, oc * 2048:(oc + 1) * 2048], ATW, q="g", sem="a1_qst")
    ctx.after("g", "s")
    ctx.load(ag_src[:, 0:4096], KVO, q="g", sem="a1_ag")
    ctx.dwait("a1_ag", "g")
    cc1 = g.collective_compute("AllGather", ALU.bypass, ins=[ag_src[:, 0:4096]],
                               outs=[ag_dst[:, 0:4096]], replica_groups=rg)
    agsem = ctx.gsem("a1_agsem")
    agsem.inc(cc1)
    agsem.wait(ctx.y)
    ctx.dwait("a1_qst", "y")

    KF = ctx.abuf(0, 4096)
    VF = ctx.abuf(4096, 4096)
    VT = ctx.abuf(8192, 4096)
    SC = ctx.abuf(12288, 4096)
    PT = ctx.abuf(16384, 4096)
    AM = ctx.abuf(20480, 4096)
    CI = ctx.abuf(24576, 4096)
    QH = ctx.abuf(28672, 128)
    AO = ctx.abuf(28800, 128)
    ctx.t_gate("y")
    ctx.after("y", "v")
    ctx.load(KF[:, 0:2048], ag_dst[0:128, 0:2048], q="y", sem="a1_kv")
    ctx.load(KF[:, 2048:4096], ag_dst[128:256, 0:2048], q="y", sem="a1_kv")
    ctx.load(VF[:, 0:2048], ag_dst[0:128, 2048:4096], q="y", sem="a1_kv")
    ctx.load(VF[:, 2048:4096], ag_dst[128:256, 2048:4096], q="y", sem="a1_kv")
    ctx.load(CI, colidx_hbm, q="y", sem="a1_kv")
    ctx.dwait("a1_kv", "t")
    ctx.dwait("a1_kv", "v")
    emit_tm_to_fm(ctx, VT, ident,
                  [(VF[:, jc * 128:(jc + 1) * 128], jc * 128) for jc in range(32)])
    ctx.after("t", "v")
    for ci in range(NTC):
        v.tensor_scalar(ST[:, 0:1], TB[:, 0:1], float(128 * ci), None, ALU.add)
        for jt in range(8):
            v.tensor_scalar(AM[:, jt * 512:(jt + 1) * 512], CI[:, jt * 512:(jt + 1) * 512],
                            ST[:, 0:1], -1e30, ALU.is_gt, ALU.mult)
        for h in range(16):
            ctx.after("y", "t")
            ctx.t_gate("y")
            ctx.load(QH, q_scr[:, h * 2048 + ci * 128: h * 2048 + (ci + 1) * 128],
                     q="y", sem="a1_qh")
            ctx.dwait("a1_qh", "t")
            for jt in range(8):
                b = ctx.bank()
                b.begin(t)
                ins = t.matmul(b.ap[:, 0:512], QH, KF[:, jt * 512:(jt + 1) * 512],
                               start=True, stop=True)
                b.fin_write(ins)
                b.begin_read("v")
                ins = v.scalar_tensor_tensor(SC[:, jt * 512:(jt + 1) * 512], b.ap[:, 0:512],
                                             SCALE, AM[:, jt * 512:(jt + 1) * 512],
                                             ALU.mult, ALU.add)
                b.fin_read(ins)
            v.tensor_reduce(ST[:, 1:2], SC, AX.X, ALU.max)
            ctx.fence("v")
            v.tensor_scalar(ST[:, 2:3], ST[:, 1:2], -1.0, None, ALU.mult)
            ctx.mark("v")
            ctx.after("s", "v")
            s.activation(SC, SC, AF.Exp, bias=ST[:, 2:3], accum_out=ST[:, 3:4])
            ctx.fence("s")
            ctx.mark("s")
            ctx.after("v", "s")
            v.reciprocal(ST[:, 4:5], ST[:, 3:4])
            ctx.mark("v")
            ctx.after("t", "v")
            ctx.after("t", "s")
            emit_tm_to_fm(ctx, PT, ident,
                          [(SC[:, jc * 128:(jc + 1) * 128], jc * 128) for jc in range(32)])
            ctx.after("t", "v")
            b = ctx.bank()
            b.begin(t)
            for jc in range(32):
                ins = t.matmul(b.ap[:, 0:128], PT[:, jc * 128:(jc + 1) * 128],
                               VT[:, jc * 128:(jc + 1) * 128],
                               start=(jc == 0), stop=(jc == 31))
            b.fin_write(ins)
            b.begin_read("s")
            ins = s.activation(AO, b.ap[:, 0:128], AF.Identity, scale=ST[:, 4:5])
            b.fin_read(ins)
            ctx.after("g", "s")
            ctx.dwait("a1_ao", "s")
            ctx.load(att_scr[ci * 128:(ci + 1) * 128, h * 128:(h + 1) * 128], AO,
                     q="g", sem="a1_ao")
    ctx.dwait("a1_ao", "y")

    AF_ = ctx.abuf(0, 8192)
    FG = ctx.abuf(8192, 8192)
    WPB = [ctx.abuf(16384, 8192), ctx.abuf(24576, 8192)]
    BPB = ctx.abuf(32768, 2048)
    BPR = ctx.abuf(43008, 2048)
    ctx.ln_off = (34816, 36864, 40960)
    ones1 = ctx.sbuf("c_ones1", [1, 128])
    ctx.after("y", "s")
    ctx.after("y", "v")
    ctx.load(BPR[0:1, 0:2048], bproj_hbm, q="y", sem="a1_bp")
    ctx.dwait("a1_bp", "t")
    for seg in range(4):
        b = ctx.bank()
        b.begin(t)
        ins = t.matmul(b.ap[:, 0:512], ones1[0:1, :], BPR[0:1, seg * 512:(seg + 1) * 512],
                       start=True, stop=True)
        b.fin_write(ins)
        b.begin_read("v")
        ins = v.tensor_copy(BPB[:, seg * 512:(seg + 1) * 512], b.ap[:, 0:512])
        b.fin_read(ins)
    wpdone = {0: [], 1: []}
    for tg in range(4):
        ctx.dwait("ln_xdst", "y")
        ctx.after("y", "v")
        for k in range(4):
            ctx.t_gate("y")
            ctx.load(XC, att_scr[(tg * 4 + k) * 128:(tg * 4 + k + 1) * 128, :], q="y", sem="a1_af")
            ctx.dwait("a1_af", "t")
            emit_tm_to_fm(ctx, AF_, ident,
                          [(XC[:, dc * 128:(dc + 1) * 128], dc * 512 + k * 128)
                           for dc in range(NDC)])
            ctx.t_gate("y")
            ctx.after("y", "v")
        ctx.after("t", "v")
        for os_ in range(4):
            wb = WPB[os_ % 2]
            for (sm, val) in wpdone[os_ % 2]:
                sm.wait(ctx.y, val)
            ctx.load(wb, wproj_hbm[os_], q="y", sem=f"a1_wp{os_ % 2}")
            ctx.dwait(f"a1_wp{os_ % 2}", "t")
            lastb = None
            for k in range(4):
                b = ctx.bank()
                b.begin(t)
                for hc in range(NDC):
                    ins = t.matmul(b.ap[:, 0:512],
                                   AF_[:, hc * 512 + k * 128: hc * 512 + (k + 1) * 128],
                                   wb[:, hc * 512:(hc + 1) * 512],
                                   start=(hc == 0), stop=(hc == NDC - 1))
                b.fin_write(ins)
                lastb = b
                b.begin_read("v")
                ins = v.scalar_tensor_tensor(
                    FG[:, k * 2048 + os_ * 512: k * 2048 + (os_ + 1) * 512],
                    b.ap[:, 0:512], 1.0, BPB[:, os_ * 512:(os_ + 1) * 512],
                    ALU.mult, ALU.add)
                b.fin_read(ins)
            wpdone[os_ % 2] = [(lastb.sw, lastb.sw.n)]
        for k in range(4):
            ctx.t_gate("y")
            ctx.after("y", "v")
            ctx.load(XC, xsrc[(tg * 4 + k) * 128:(tg * 4 + k + 1) * 128, :], q="y", sem="a1_xc2")
            ctx.dwait("a1_xc2", "v")
            emit_ln_writeback(ctx, XC, FG[:, k * 2048:(k + 1) * 2048], xdst, tg * 4 + k)
    ctx.barrier("t", "v", "s", "g")



def emit_l2attn(ctx, xsrc, xdst, wq_hbm, wk_hbm, wv_hbm, wo_hbm, qnw_hbm, knw_hbm,
                q_scr, k_scr, v_scr, att_scr, ag_src, ag_dst, ident, isodd,
                cosq_hbm, sinq_hbm, rot_hbm, colidx_hbm, prow_hbm):
    v, s, t, g = ctx.v, ctx.s, ctx.t, ctx.g
    rg = [[0, 1], [2, 3], [4, 5], [6, 7]]
    DH_S = float(128 ** -0.5)
    W2 = 2560
    XFF = ctx.abuf(0, 32768)
    WB = [ctx.abuf(32768, 2048), ctx.abuf(34816, 2048)]
    PRJ = ctx.abuf(36864, 2048)
    XC = ctx.abuf(38912, 2048)
    ROT = ctx.abuf(38912, 2048)
    NRM = ctx.abuf(40960, 2048)
    CS2 = ctx.abuf(43008, 2048)
    SIN = ctx.abuf(45056, 2048)
    NW = ctx.sbuf("l2a_nw", [128, 2])
    MSK = ctx.sbuf("c_mask", [128, 1])
    ROTM = ctx.sbuf("l2a_rot", [128, 128])
    ctx.load(ROTM[:, :], rot_hbm, q="y", sem="a2_c")
    ctx.load(MSK[0:1, 0:1], isodd, q="y", sem="a2_c")
    ctx.dwait("a2_c", "v")
    emit_bcast_col(ctx, MSK[0:1, 0:1], MSK[:, 0:1], 1, ident)

    for tc in range(NTC):
        ctx.t_gate("y")
        ctx.after("y", "v")
        ctx.load(XC, xsrc[tc * 128:(tc + 1) * 128, :], q="y", sem="a2_xc")
        ctx.dwait("a2_xc", "t")
        emit_tm_to_fm(ctx, XFF, ident,
                      [(XC[:, dc * 128:(dc + 1) * 128], dc * 2048 + tc * 128)
                       for dc in range(NDC)])
    ctx.after("t", "v")
    ctx.load(CS2, cosq_hbm, q="y", sem="a2_c2")
    ctx.load(SIN, sinq_hbm, q="y", sem="a2_c2")
    ctx.dwait("a2_c2", "v")

    qbdone = {0: [], 1: []}
    def proj_head(oc, w_hbm_i, nw_hbm, dst_q_scr_col, kv_slot):
        wb = WB[oc % 2]
        for (sm, val) in qbdone[oc % 2]:
            sm.wait(ctx.y, val)
        ctx.after("y", "v")
        ctx.load(wb, w_hbm_i, q="y", sem=f"a2_w{oc % 2}")
        if nw_hbm is not None:
            ctx.load(NW[:, 0:1], nw_hbm, q="y", sem=f"a2_w{oc % 2}")
        ctx.dwait(f"a2_w{oc % 2}", "t")
        ctx.dwait(f"a2_w{oc % 2}", "v")
        lastb = None
        for tt4 in range(4):
            b = ctx.bank()
            b.begin(t)
            for hc in range(NDC):
                ins = t.matmul(b.ap[:, 0:512], wb[:, hc * 128:(hc + 1) * 128],
                               XFF[:, hc * 2048 + tt4 * 512: hc * 2048 + (tt4 + 1) * 512],
                               start=(hc == 0), stop=(hc == NDC - 1))
            b.fin_write(ins)
            lastb = b
            b.begin_read("v")
            ins = v.tensor_copy(PRJ[:, tt4 * 512:(tt4 + 1) * 512], b.ap[:, 0:512])
            b.fin_read(ins)
        qbdone[oc % 2] = [(lastb.sw, lastb.sw.n)]
        ctx.dwait("a2_qst", "v")
        ctx.dwait("a2_kst", "v")
        v.tensor_tensor(ROT, PRJ, PRJ, ALU.mult)
        ctx.after("t", "v")
        ones1 = ctx.sbuf("c_ones1", [1, 128])
        for tt4 in range(4):
            b = ctx.bank()
            b.begin(t)
            onescol = ctx.sbuf("c_onescol", [128, 1])
            ins = t.matmul(b.ap[0:1, 0:512], onescol[:, 0:1], ROT[:, tt4 * 512:(tt4 + 1) * 512],
                           start=True, stop=True)
            b.fin_write(ins)
            b.begin_read("v")
            r0 = NRM[0:1, tt4 * 512:(tt4 + 1) * 512]
            ins = v.tensor_scalar(r0, b.ap[0:1, 0:512], 1.0 / 128.0, 1e-6, ALU.mult, ALU.add)
            b.fin_read(ins)
        ctx.mark("v")
        ctx.after("s", "v")
        wbo = WB[(oc + 1) % 2]
        s.activation(ROT[0:1, 0:2048], NRM[0:1, 0:2048], AF.Sqrt)
        ctx.mark("s")
        ctx.after("v", "s")
        v.reciprocal(ROT[0:1, 0:2048], ROT[0:1, 0:2048])
        v.tensor_tensor(wbo[0:1, 0:2048], ROT[0:1, 0:2048], ROT[0:1, 0:2048], ALU.mult)
        v.tensor_tensor(wbo[0:1, 0:2048], wbo[0:1, 0:2048], NRM[0:1, 0:2048], ALU.mult)
        v.tensor_scalar(wbo[0:1, 0:2048], wbo[0:1, 0:2048], -0.5, 1.5, ALU.mult, ALU.add)
        ins = v.tensor_tensor(ROT[0:1, 0:2048], ROT[0:1, 0:2048], wbo[0:1, 0:2048], ALU.mult)
        ctx.mark("v")
        ctx.after("t", "v")
        for tt4 in range(4):
            b = ctx.bank()
            b.begin(t)
            ins = t.matmul(b.ap[:, 0:512], ones1[0:1, :], ROT[0:1, tt4 * 512:(tt4 + 1) * 512],
                           start=True, stop=True)
            b.fin_write(ins)
            b.begin_read("v")
            ins = v.tensor_copy(NRM[:, tt4 * 512:(tt4 + 1) * 512], b.ap[:, 0:512])
            b.fin_read(ins)
        v.tensor_scalar(NW[:, 1:2], NW[:, 0:1], 1.0,
                        DH_S if kv_slot is None else 1.0, ALU.add, ALU.mult)
        v.tensor_tensor(PRJ, PRJ, NRM, ALU.mult)
        v.tensor_scalar(PRJ, PRJ, NW[:, 1:2], None, ALU.mult)
        ctx.after("t", "v")
        for tt4 in range(4):
            b = ctx.bank()
            b.begin(t)
            ins = t.matmul(b.ap[:, 0:512], ROTM[:, :], PRJ[:, tt4 * 512:(tt4 + 1) * 512],
                           start=True, stop=True)
            b.fin_write(ins)
            b.begin_read("v")
            ins = v.tensor_tensor(ROT[:, tt4 * 512:(tt4 + 1) * 512], b.ap[:, 0:512],
                                  SIN[:, tt4 * 512:(tt4 + 1) * 512], ALU.mult)
            b.fin_read(ins)
        v.tensor_tensor(PRJ, PRJ, CS2, ALU.mult)
        ins = v.tensor_tensor(PRJ, PRJ, ROT, ALU.add)
        ctx.mark("v")
        if kv_slot is None:
            ctx.after("g", "v")
            ctx.load(q_scr[:, dst_q_scr_col:dst_q_scr_col + 2048], PRJ, q="g", sem="a2_qst")
        else:
            ctx.after("g", "v")
            ctx.load(ag_src[:, kv_slot * 512:(kv_slot + 1) * 512], PRJ[:, 1536:2048],
                     q="g", sem="a2_kst")
            ctx.load(k_scr[:, kv_slot * 2048:(kv_slot + 1) * 2048], PRJ, q="g", sem="a2_kst")
        return

    for oc in range(16):
        proj_head(oc, wq_hbm[oc], qnw_hbm, oc * 2048, None)
    for kc in range(4):
        proj_head(16 + kc, wk_hbm[kc], knw_hbm, 0, kc)
    for vc in range(4):
        wb = WB[vc % 2]
        for (sm, val) in qbdone[vc % 2]:
            sm.wait(ctx.y, val)
        ctx.after("y", "v")
        ctx.load(wb, wv_hbm[vc], q="y", sem=f"a2_w{vc % 2}")
        ctx.dwait(f"a2_w{vc % 2}", "t")
        ctx.dwait("a2_vst", "v")
        ctx.dwait("a2_kst", "v")
        lastb = None
        for tt4 in range(4):
            b = ctx.bank()
            b.begin(t)
            for hc in range(NDC):
                ins = t.matmul(b.ap[:, 0:512], wb[:, hc * 128:(hc + 1) * 128],
                               XFF[:, hc * 2048 + tt4 * 512: hc * 2048 + (tt4 + 1) * 512],
                               start=(hc == 0), stop=(hc == NDC - 1))
            b.fin_write(ins)
            lastb = b
            b.begin_read("v")
            ins = v.tensor_copy(PRJ[:, tt4 * 512:(tt4 + 1) * 512], b.ap[:, 0:512])
            b.fin_read(ins)
        qbdone[vc % 2] = [(lastb.sw, lastb.sw.n)]
        ctx.after("g", "v")
        ctx.load(ag_src[:, 2048 + vc * 512:2048 + (vc + 1) * 512], PRJ[:, 1536:2048],
                 q="g", sem="a2_vst")
        ctx.load(v_scr[:, vc * 2048:(vc + 1) * 2048], PRJ, q="g", sem="a2_vst")

    ctx.dwait("a2_kst", "g")
    ctx.dwait("a2_vst", "g")
    cc1 = g.collective_compute("AllGather", ALU.bypass, ins=[ag_src[:, 0:4096]],
                               outs=[ag_dst[:, 0:4096]], replica_groups=rg)
    agsem = ctx.gsem("a2_agsem")
    agsem.inc(cc1)
    agsem.wait(ctx.y)
    ctx.dwait("a2_qst", "y")
    ctx.dwait("a2_kst", "y")
    ctx.dwait("a2_vst", "y")

    W2 = 2560
    KF = ctx.abuf(0, 4 * W2)
    VF = ctx.abuf(10240, 4 * W2)
    VT = ctx.abuf(20480, 4 * W2)
    SC = ctx.abuf(30720, W2)
    PT = ctx.abuf(33280, W2)
    AM = ctx.abuf(35840, W2)
    CI = ctx.abuf(38400, W2)
    QH = ctx.abuf(40960, 128)
    AO = ctx.abuf(41088, 128)
    ST = ctx.sbuf("l1_st", [128, 8])
    PRW = ctx.sbuf("l2a_prw", [128, 4])
    ctx.t_gate("y")
    ctx.after("y", "v")
    ctx.after("y", "s")
    for h4 in range(4):
        ctx.load(KF[:, h4 * W2:h4 * W2 + 512], ag_dst[0:128, h4 * 512:(h4 + 1) * 512],
                 q="y", sem="a2_kv")
        ctx.load(KF[:, h4 * W2 + 512:(h4 + 1) * W2], k_scr[:, h4 * 2048:(h4 + 1) * 2048],
                 q="y", sem="a2_kv")
        ctx.load(VF[:, h4 * W2:h4 * W2 + 512], ag_dst[0:128, 2048 + h4 * 512:2048 + (h4 + 1) * 512],
                 q="y", sem="a2_kv")
        ctx.load(VF[:, h4 * W2 + 512:(h4 + 1) * W2], v_scr[:, h4 * 2048:(h4 + 1) * 2048],
                 q="y", sem="a2_kv")
    ctx.load(CI, colidx_hbm, q="y", sem="a2_kv")
    ctx.load(PRW[:, 0:1], prow_hbm, q="y", sem="a2_kv")
    ctx.dwait("a2_kv", "t")
    ctx.dwait("a2_kv", "v")
    emit_tm_to_fm(ctx, VT, ident,
                  [(VF[:, jc * 128:(jc + 1) * 128], jc * 128) for jc in range(80)])
    ctx.after("t", "v")
    v.tensor_scalar(PRW[:, 3:4], MSK[:, 0:1], -1.0, 1e30, ALU.add, ALU.mult)
    for ci in range(NTC):
        v.tensor_scalar(PRW[:, 1:2], PRW[:, 0:1], float(128 * ci), None, ALU.add)
        v.tensor_scalar(PRW[:, 2:3], PRW[:, 1:2], 512.0, None, ALU.add)
        for jt in range(5):
            sl = slice(jt * 512, (jt + 1) * 512)
            v.tensor_scalar(AM[:, sl], CI[:, sl], PRW[:, 2:3], -1e30, ALU.is_gt, ALU.mult)
            v.tensor_scalar(SC[:, sl], CI[:, sl], PRW[:, 1:2], -1e30, ALU.is_lt, ALU.mult)
            v.tensor_tensor(AM[:, sl], AM[:, sl], SC[:, sl], ALU.add)
        v.tensor_scalar(AM[:, 0:512], AM[:, 0:512], PRW[:, 3:4], None, ALU.add)
        for h in range(16):
            h4 = h // 4
            ctx.t_gate("y")
            ctx.load(QH, q_scr[:, h * 2048 + ci * 128: h * 2048 + (ci + 1) * 128],
                     q="y", sem="a2_qh")
            ctx.dwait("a2_qh", "t")
            for jt in range(5):
                b = ctx.bank()
                b.begin(t)
                ins = t.matmul(b.ap[:, 0:512], QH, KF[:, h4 * W2 + jt * 512: h4 * W2 + (jt + 1) * 512],
                               start=True, stop=True)
                b.fin_write(ins)
                b.begin_read("v")
                ins = v.scalar_tensor_tensor(SC[:, jt * 512:(jt + 1) * 512], b.ap[:, 0:512],
                                             1.0, AM[:, jt * 512:(jt + 1) * 512],
                                             ALU.mult, ALU.add)
                b.fin_read(ins)
            v.tensor_reduce(ST[:, 1:2], SC, AX.X, ALU.max)
            ctx.fence("v")
            v.tensor_scalar(ST[:, 2:3], ST[:, 1:2], -1.0, None, ALU.mult)
            ctx.mark("v")
            ctx.after("s", "v")
            s.activation(SC, SC, AF.Exp, bias=ST[:, 2:3], accum_out=ST[:, 3:4])
            ctx.fence("s")
            ctx.mark("s")
            ctx.after("v", "s")
            v.reciprocal(ST[:, 4:5], ST[:, 3:4])
            ctx.mark("v")
            ctx.after("t", "v")
            ctx.after("t", "s")
            emit_tm_to_fm(ctx, PT, ident,
                          [(SC[:, jc * 128:(jc + 1) * 128], jc * 128) for jc in range(20)])
            ctx.after("t", "v")
            b = ctx.bank()
            b.begin(t)
            for jc in range(20):
                ins = t.matmul(b.ap[:, 0:128], PT[:, jc * 128:(jc + 1) * 128],
                               VT[:, h4 * W2 + jc * 128: h4 * W2 + (jc + 1) * 128],
                               start=(jc == 0), stop=(jc == 19))
            b.fin_write(ins)
            b.begin_read("s")
            ins = s.activation(AO, b.ap[:, 0:128], AF.Identity, scale=ST[:, 4:5])
            b.fin_read(ins)
            ctx.after("g", "s")
            ctx.dwait("a2_ao", "s")
            ctx.load(att_scr[ci * 128:(ci + 1) * 128, h * 128:(h + 1) * 128], AO,
                     q="g", sem="a2_ao")
    ctx.dwait("a2_ao", "y")

    AF_ = ctx.abuf(0, 8192)
    FG = ctx.abuf(8192, 8192)
    WPB = [ctx.abuf(16384, 8192), ctx.abuf(24576, 8192)]
    XC2 = ctx.abuf(38912, 2048)
    ctx.ln_off = (32768, 34816, 36864)
    wpdone = {0: [], 1: []}
    for tg in range(4):
        ctx.dwait("ln_xdst", "y")
        ctx.after("y", "v")
        ctx.after("y", "s")
        for k in range(4):
            ctx.t_gate("y")
            ctx.load(XC2, att_scr[(tg * 4 + k) * 128:(tg * 4 + k + 1) * 128, :], q="y", sem="a2_af")
            ctx.dwait("a2_af", "t")
            emit_tm_to_fm(ctx, AF_, ident,
                          [(XC2[:, dc * 128:(dc + 1) * 128], dc * 512 + k * 128)
                           for dc in range(NDC)])
            ctx.t_gate("y")
            ctx.after("y", "v")
        ctx.after("t", "v")
        for os_ in range(4):
            wb = WPB[os_ % 2]
            for (sm, val) in wpdone[os_ % 2]:
                sm.wait(ctx.y, val)
            ctx.load(wb, wo_hbm[os_], q="y", sem=f"a2_wp{os_ % 2}")
            ctx.dwait(f"a2_wp{os_ % 2}", "t")
            lastb = None
            for k in range(4):
                b = ctx.bank()
                b.begin(t)
                for hc in range(NDC):
                    ins = t.matmul(b.ap[:, 0:512],
                                   AF_[:, hc * 512 + k * 128: hc * 512 + (k + 1) * 128],
                                   wb[:, hc * 512:(hc + 1) * 512],
                                   start=(hc == 0), stop=(hc == NDC - 1))
                b.fin_write(ins)
                lastb = b
                b.begin_read("v")
                ins = v.tensor_copy(FG[:, k * 2048 + os_ * 512: k * 2048 + (os_ + 1) * 512],
                                    b.ap[:, 0:512])
                b.fin_read(ins)
            wpdone[os_ % 2] = [(lastb.sw, lastb.sw.n)]
        for k in range(4):
            ctx.t_gate("y")
            ctx.after("y", "v")
            ctx.load(XC2, xsrc[(tg * 4 + k) * 128:(tg * 4 + k + 1) * 128, :], q="y", sem="a2_xc2")
            ctx.dwait("a2_xc2", "v")
            emit_ln_writeback(ctx, XC2, FG[:, k * 2048:(k + 1) * 2048], xdst, tg * 4 + k)
    ctx.barrier("t", "v", "s", "g")



def emit_l0attn(ctx, xsrc, xdst, wqkv_hbm, wgate_hbm, wout_hbm, nw_hbm,
                ddt_hbm, qd_hbm, kd_hbm, bd_hbm,
                qkv_scr, att_scr, ag_src, ag_dst, ident, isodd):
    v, s, t, g = ctx.v, ctx.s, ctx.t, ctx.g
    rg = [[0, 1], [2, 3], [4, 5], [6, 7]]
    XFF = ctx.abuf(0, 32768)
    WB = [ctx.abuf(32768, 2048), ctx.abuf(34816, 2048)]
    PRJ = ctx.abuf(36864, 2048)
    XC = ctx.abuf(38912, 2048)
    MSK = ctx.sbuf("c_mask", [128, 1])
    ctx.load(MSK[0:1, 0:1], isodd, q="y", sem="a0_msk")
    ctx.dwait("a0_msk", "v")
    emit_bcast_col(ctx, MSK[0:1, 0:1], MSK[:, 0:1], 1, ident)

    for tc in range(NTC):
        ctx.t_gate("y")
        ctx.after("y", "v")
        ctx.load(XC, xsrc[tc * 128:(tc + 1) * 128, :], q="y", sem="a0_xc")
        ctx.dwait("a0_xc", "t")
        emit_tm_to_fm(ctx, XFF, ident,
                      [(XC[:, dc * 128:(dc + 1) * 128], dc * 2048 + tc * 128)
                       for dc in range(NDC)])
    ctx.after("t", "v")

    qbdone = {0: [], 1: []}
    for oc in range(48):
        wb = WB[oc % 2]
        for (sm, val) in qbdone[oc % 2]:
            sm.wait(ctx.y, val)
        ctx.load(wb, wqkv_hbm[oc], q="y", sem=f"a0_w{oc % 2}")
        ctx.dwait(f"a0_w{oc % 2}", "t")
        ctx.dwait("a0_qst", "s")
        lastb = None
        for tt4 in range(4):
            b = ctx.bank()
            b.begin(t)
            for hc in range(NDC):
                ins = t.matmul(b.ap[:, 0:512], wb[:, hc * 128:(hc + 1) * 128],
                               XFF[:, hc * 2048 + tt4 * 512: hc * 2048 + (tt4 + 1) * 512],
                               start=(hc == 0), stop=(hc == NDC - 1))
            b.fin_write(ins)
            lastb = b
            b.begin_read("s")
            ins = s.activation(PRJ[:, tt4 * 512:(tt4 + 1) * 512], b.ap[:, 0:512], AF.Silu)
            b.fin_read(ins)
        qbdone[oc % 2] = [(lastb.sw, lastb.sw.n)]
        ctx.after("g", "s")
        ctx.load(qkv_scr[:, oc * 2048:(oc + 1) * 2048], PRJ, q="g", sem="a0_qst")
    ctx.dwait("a0_qst", "y")

    ATT = ctx.abuf(0, 32768)
    QH0 = ctx.abuf(32768, 2048)
    KH0 = ctx.abuf(34816, 2048)
    VH0 = ctx.abuf(36864, 2048)
    DDT = ctx.abuf(38912, 512)
    AWT = ctx.abuf(39424, 512)
    VT2 = ctx.abuf(39936, 128)
    KT2 = ctx.abuf(40064, 128)
    STATE = ctx.abuf(40192, 64)
    SFIN = ctx.abuf(40256, 2048)
    YB = ctx.abuf(42304, 64)
    CC = ctx.sbuf("l0_cc", [128, 8])
    BDC = ctx.sbuf("l0_bd", [128, 4])

    def qkv_slice(h, which):
        f0 = 192 * h + 64 * which
        oc, po = f0 // 128, f0 % 128
        return qkv_scr[po:po + 64, oc * 2048:(oc + 1) * 2048]

    for h in range(32):
        ctx.t_gate("y")
        ctx.after("y", "v")
        ctx.after("y", "s")
        ctx.load(QH0[0:64, :], qkv_slice(h, 0), q="y", sem="a0_h")
        ctx.load(KH0[0:64, :], qkv_slice(h, 1), q="y", sem="a0_h")
        ctx.load(VH0[0:64, :], qkv_slice(h, 2), q="y", sem="a0_h")
        ctx.load(DDT[:, 0:256], ddt_hbm[h][0], q="y", sem="a0_h")
        ctx.load(DDT[:, 256:512], ddt_hbm[h][1], q="y", sem="a0_h")
        ctx.load(CC[:, 0:1], qd_hbm[h][0], q="y", sem="a0_h")
        ctx.load(CC[:, 1:2], qd_hbm[h][1], q="y", sem="a0_h")
        ctx.load(CC[:, 2:3], kd_hbm[h][0], q="y", sem="a0_h")
        ctx.load(CC[:, 3:4], kd_hbm[h][1], q="y", sem="a0_h")
        ctx.load(BDC[0:64, 0:1], bd_hbm[h], q="y", sem="a0_h")
        ctx.dwait("a0_h", "t")
        ctx.dwait("a0_h", "v")
        ctx.dwait("a0_h", "s")
        v.memset(STATE[0:64, :], 0.0)
        ctx.mark("v")
        ctx.after("t", "v")
        for c in range(8):
            qs = slice(c * 256, (c + 1) * 256)
            if c:
                ctx.t_gate("v")
            for jc in range(2):
                b = ctx.bank()
                b.begin(t)
                ins = t.transpose(b.ap[:, 0:64],
                                  VH0[0:64, c * 256 + jc * 128: c * 256 + (jc + 1) * 128],
                                  ident[0:64, 0:64])
                b.fin_write(ins)
                b.begin_read("v")
                ins = v.tensor_copy(VT2[:, jc * 64:(jc + 1) * 64], b.ap[:, 0:64])
                b.fin_read(ins)
                b2 = ctx.bank()
                b2.begin(t)
                ins = t.transpose(b2.ap[:, 0:64],
                                  KH0[0:64, c * 256 + jc * 128: c * 256 + (jc + 1) * 128],
                                  ident[0:64, 0:64])
                b2.fin_write(ins)
                b2.begin_read("v")
                ins = v.tensor_scalar(KT2[:, jc * 64:(jc + 1) * 64], b2.ap[:, 0:64],
                                      CC[:, 2 + jc:3 + jc], None, ALU.mult)
                b2.fin_read(ins)
            ctx.after("t", "v")
            for jc in range(2):
                b = ctx.bank()
                b.begin(t)
                ins = t.matmul(b.ap[:, 0:256],
                               KH0[0:64, c * 256 + jc * 128: c * 256 + (jc + 1) * 128],
                               QH0[0:64, qs], start=True, stop=True)
                b.fin_write(ins)
                b.begin_read("v")
                ins = v.tensor_tensor(AWT[:, jc * 256:(jc + 1) * 256], b.ap[:, 0:256],
                                      DDT[:, jc * 256:(jc + 1) * 256], ALU.mult)
                b.fin_read(ins)
            ctx.after("t", "v")
            for ic in range(2):
                bA = ctx.bank()
                bA.begin(t)
                for jc in range(2):
                    insA = t.matmul(bA.ap[:, 0:64],
                                    AWT[:, jc * 256 + ic * 128: jc * 256 + (ic + 1) * 128],
                                    VT2[:, jc * 64:(jc + 1) * 64],
                                    start=(jc == 0), stop=(jc == 1))
                bA.fin_write(insA)
                bB = ctx.bank()
                bB.begin(t)
                insB = t.matmul(bB.ap[:, 0:64], QH0[0:64, c * 256 + ic * 128: c * 256 + (ic + 1) * 128],
                                STATE[0:64, :], start=True, stop=True)
                bB.fin_write(insB)
                bB.begin_read("s")
                ins = s.activation(YB, bB.ap[:, 0:64], AF.Identity, scale=CC[:, ic:ic + 1])
                bB.fin_read(ins)
                ctx.mark("s")
                ctx.after("v", "s")
                bA.begin_read("v")
                tcg = 2 * c + ic
                ins = v.tensor_tensor(ATT[:, tcg * 2048 + 64 * h: tcg * 2048 + 64 * h + 64],
                                      bA.ap[:, 0:64], YB, ALU.add)
                bA.fin_read(ins)
            bS = ctx.bank()
            bS.begin(t)
            for jc in range(2):
                insS = t.matmul(bS.ap[0:64, 0:64], KT2[:, jc * 64:(jc + 1) * 64],
                                VT2[:, jc * 64:(jc + 1) * 64],
                                start=(jc == 0), stop=(jc == 1))
            bS.fin_write(insS)
            bS.begin_read("v")
            ins = v.scalar_tensor_tensor(STATE[0:64, :], STATE[0:64, :], BDC[0:64, 0:1],
                                         bS.ap[0:64, 0:64], ALU.mult, ALU.add)
            bS.fin_read(ins)
            ctx.mark("v")
            ctx.after("t", "v")
        v.tensor_copy(SFIN[0:64, h * 64:(h + 1) * 64], STATE[0:64, :])
        ctx.mark("v")
    v.memset(QH0[0:64, :], 0.0)
    ctx.mark("v")
    ctx.after("g", "v")
    ctx.load(ag_src[0:64, 0:2048], SFIN[0:64, 0:2048], q="g", sem="a0_ag")
    ctx.load(ag_src[64:128, 0:2048], QH0[0:64, :], q="g", sem="a0_ag")
    ctx.dwait("a0_ag", "g")
    cc1 = g.collective_compute("AllGather", ALU.bypass, ins=[ag_src[:, 0:2048]],
                               outs=[ag_dst[:, 0:2048]], replica_groups=rg)
    agsem = ctx.gsem("a0_agsem")
    agsem.inc(cc1)
    agsem.wait(ctx.y)
    RS = ctx.abuf(40256, 2048)
    ctx.after("y", "v")
    ctx.load(RS[0:64, 0:2048], ag_dst[0:64, 0:2048], q="y", sem="a0_rs")
    ctx.dwait("a0_rs", "v")
    v.tensor_scalar(RS[0:64, :], RS[0:64, :], MSK[0:64, 0:1], None, ALU.mult)
    ctx.mark("v")

    for h in range(32):
        ctx.t_gate("y")
        ctx.after("y", "v")
        ctx.load(QH0[0:64, :], qkv_slice(h, 0), q="y", sem="a0_h2")
        ctx.load(CC[:, 0:1], qd_hbm[h][0], q="y", sem="a0_h2")
        ctx.load(CC[:, 1:2], qd_hbm[h][1], q="y", sem="a0_h2")
        ctx.load(BDC[0:64, 1:2], bd_hbm[h], q="y", sem="a0_h2")
        ctx.dwait("a0_h2", "t")
        ctx.dwait("a0_h2", "v")
        v.tensor_scalar(CC[:, 4:5], CC[:, 0:1], 1.0, None, ALU.mult)
        v.tensor_scalar(CC[:, 5:6], CC[:, 1:2], 1.0, None, ALU.mult)
        emit_bcast_col(ctx, BDC[0:1, 1:2], BDC[:, 2:3], 1, ident)
        ctx.mark("v")
        ctx.after("t", "v")
        for c in range(8):
            for ic in range(2):
                bB = ctx.bank()
                bB.begin(t)
                insB = t.matmul(bB.ap[:, 0:64],
                                QH0[0:64, c * 256 + ic * 128: c * 256 + (ic + 1) * 128],
                                RS[0:64, h * 64:(h + 1) * 64], start=True, stop=True)
                bB.fin_write(insB)
                bB.begin_read("v")
                tcg = 2 * c + ic
                sl = slice(tcg * 2048 + 64 * h, tcg * 2048 + 64 * h + 64)
                ins = v.scalar_tensor_tensor(ATT[:, sl], bB.ap[:, 0:64], CC[:, 4 + ic:5 + ic],
                                             ATT[:, sl], ALU.mult, ALU.add)
                bB.fin_read(ins)
            v.tensor_scalar(CC[:, 4:6], CC[:, 4:6], BDC[:, 2:3], None, ALU.mult)
            ctx.mark("v")
            ctx.after("t", "v")
    NWB = ctx.abuf(32768, 2048)
    NWR = ctx.abuf(34816, 2048)
    ST3 = ctx.sbuf("l1_st", [128, 8])
    ones1 = ctx.sbuf("c_ones1", [1, 128])
    ctx.after("y", "v")
    ctx.load(NWR[0:1, 0:2048], nw_hbm, q="y", sem="a0_nw")
    ctx.dwait("a0_nw", "t")
    for seg in range(4):
        b = ctx.bank()
        b.begin(t)
        ins = t.matmul(b.ap[:, 0:512], ones1[0:1, :], NWR[0:1, seg * 512:(seg + 1) * 512],
                       start=True, stop=True)
        b.fin_write(ins)
        b.begin_read("v")
        ins = v.tensor_copy(NWB[:, seg * 512:(seg + 1) * 512], b.ap[:, 0:512])
        b.fin_read(ins)
    SQ3 = ctx.abuf(36864, 2048)
    for tcg in range(NTC):
        sl = slice(tcg * 2048, (tcg + 1) * 2048)
        ctx.after("s", "v")
        s.activation(SQ3, ATT[:, sl], AF.Square, accum_out=ST3[:, 0:1])
        ctx.fence("s")
        ctx.mark("s")
        ctx.after("v", "s")
        v.tensor_scalar(ST3[:, 1:2], ST3[:, 0:1], 1.0 / 2048.0, None, ALU.mult)
        tmp3 = ctx.sbuf("ln_tmp", [128, 4])
        emit_rsqrt(ctx, ST3[:, 2:3], ST3[:, 1:2], 1e-5, tmp3)
        v.tensor_scalar(ATT[:, sl], ATT[:, sl], ST3[:, 2:3], None, ALU.mult)
        v.tensor_tensor(ATT[:, sl], ATT[:, sl], NWB, ALU.mult)
        ctx.mark("v")
        ctx.after("g", "v")
        ctx.dwait("a0_att", "v")
        ctx.load(att_scr[tcg * 128:(tcg + 1) * 128, :], ATT[:, sl], q="g", sem="a0_att")
    ctx.dwait("a0_att", "y")

    XF4 = ctx.abuf(0, 8192)
    AFM = ctx.abuf(8192, 8192)
    FG = ctx.abuf(16384, 8192)
    WPB = [ctx.abuf(24576, 8192), ctx.abuf(32768, 8192)]
    GS = ctx.abuf(40960, 512)
    XC4 = ctx.abuf(41472, 2048)
    ctx.ln_off = (43520, 8192, 41472)
    wpdone = {0: [], 1: []}
    for tg in range(4):
        ctx.dwait("ln_xdst", "y")
        ctx.after("y", "v")
        ctx.after("y", "s")
        for k in range(4):
            ctx.t_gate("y")
            ctx.load(XC4, xsrc[(tg * 4 + k) * 128:(tg * 4 + k + 1) * 128, :], q="y", sem="a0_x4")
            ctx.dwait("a0_x4", "t")
            emit_tm_to_fm(ctx, XF4, ident,
                          [(XC4[:, dc * 128:(dc + 1) * 128], dc * 512 + k * 128)
                           for dc in range(NDC)])
            ctx.t_gate("y")
            ctx.after("y", "v")
            ctx.load(XC4, att_scr[(tg * 4 + k) * 128:(tg * 4 + k + 1) * 128, :], q="y", sem="a0_x4")
            ctx.dwait("a0_x4", "t")
            emit_tm_to_fm(ctx, AFM, ident,
                          [(XC4[:, dc * 128:(dc + 1) * 128], dc * 512 + k * 128)
                           for dc in range(NDC)])
        ctx.after("t", "v")
        for oc in range(16):
            wb = WPB[oc % 2]
            for (sm, val) in wpdone[oc % 2]:
                sm.wait(ctx.y, val)
            ctx.load(wb[:, 0:2048], wgate_hbm[oc], q="y", sem=f"a0_gw{oc % 2}")
            ctx.dwait(f"a0_gw{oc % 2}", "t")
            b = ctx.bank()
            b.begin(t)
            for hc in range(NDC):
                ins = t.matmul(b.ap[:, 0:512], wb[:, hc * 128:(hc + 1) * 128],
                               XF4[:, hc * 512:(hc + 1) * 512],
                               start=(hc == 0), stop=(hc == NDC - 1))
            b.fin_write(ins)
            wpdone[oc % 2] = [(b.sw, b.sw.n)]
            b.begin_read("s")
            ins = s.activation(GS, b.ap[:, 0:512], AF.Sigmoid)
            b.fin_read(ins)
            ctx.mark("s")
            ctx.after("v", "s")
            v.tensor_tensor(AFM[:, oc * 512:(oc + 1) * 512], AFM[:, oc * 512:(oc + 1) * 512],
                            GS, ALU.mult)
            ctx.mark("v")
            ctx.after("s", "v")
        ctx.after("t", "v")
        for os_ in range(4):
            wb = WPB[os_ % 2]
            for (sm, val) in wpdone[os_ % 2]:
                sm.wait(ctx.y, val)
            ctx.after("y", "s")
            ctx.load(wb, wout_hbm[os_], q="y", sem=f"a0_wp{os_ % 2}")
            ctx.dwait(f"a0_wp{os_ % 2}", "t")
            lastb = None
            for k in range(4):
                b = ctx.bank()
                b.begin(t)
                for hc in range(NDC):
                    ins = t.matmul(b.ap[:, 0:512],
                                   AFM[:, hc * 512 + k * 128: hc * 512 + (k + 1) * 128],
                                   wb[:, hc * 512:(hc + 1) * 512],
                                   start=(hc == 0), stop=(hc == NDC - 1))
                b.fin_write(ins)
                lastb = b
                b.begin_read("v")
                ins = v.tensor_copy(FG[:, k * 2048 + os_ * 512: k * 2048 + (os_ + 1) * 512],
                                    b.ap[:, 0:512])
                b.fin_read(ins)
            wpdone[os_ % 2] = [(lastb.sw, lastb.sw.n)]
        for k in range(4):
            ctx.t_gate("y")
            ctx.after("y", "v")
            ctx.after("y", "s")
            ctx.load(XC4, xsrc[(tg * 4 + k) * 128:(tg * 4 + k + 1) * 128, :], q="y", sem="a0_x5")
            ctx.dwait("a0_x5", "v")
            ctx.dwait("a0_x5", "s")
            emit_ln_writeback(ctx, XC4, FG[:, k * 2048:(k + 1) * 2048], xdst, tg * 4 + k)
    ctx.barrier("t", "v", "s", "g")



"""Fused single-launch megakernel for mk_seq_postln_d4 (entry: kernel(**inputs)).

Self-contained assembly: imports mk_kernel emitters (concatenated into
kernel.py for delivery). 8 cores; pair p owns sequence p; even core tokens
[0,2048), odd [2048,4096). Cross-core via pair AllGather (fp64-sim abstains
on collectives by design; all other math replays exactly).
"""
_CACHE = {}

H0, D0, BS0 = 32, 64, 256


def _host_decay():
    base = 1.0 / 2.0 ** (8.0 / H0)
    rate = base ** (np.arange(H0, dtype=np.float64) + 1.0)
    factor = 1.0 - 0.0 / (H0 - 1 + 1e-05) + 1e-05
    slope = rate * factor
    r = np.arange(BS0, dtype=np.float64) + 1.0
    qd = np.exp(-slope[:, None] * r[None, :]).astype(np.float32)
    kd = np.exp(-slope[:, None] * (BS0 - r[None, :])).astype(np.float32)
    dd = np.zeros((H0, BS0, BS0), np.float64)
    d = r[:, None] - r[None, :]
    for h in range(H0):
        dd[h] = np.where(d >= 0, np.exp(-slope[h] * d), 0.0)
    dd = dd.astype(np.float32)
    bd = np.exp(-slope * BS0).astype(np.float32)
    return qd, kd, dd, bd


def _host_rope(own):
    inv = (1.0 / 10000.0 ** (np.arange(0, 128, 2, dtype=np.float32) / np.float32(128))).astype(np.float32)
    pos = np.arange(own, own + 2048, dtype=np.float32)
    fr = (pos[:, None] * inv[None, :]).astype(np.float32)
    emb = np.concatenate([fr, fr], -1)
    return (np.cos(emb.astype(np.float64)).astype(np.float32).T.copy(),
            np.sin(emb.astype(np.float64)).astype(np.float32).T.copy())


def _tables():
    if "taps" in _CACHE:
        return _CACHE
    taps, w0 = build_spline_tables()
    _CACHE["taps"] = taps
    _CACHE["w0"] = w0
    qd, kd, dd, bd = _host_decay()
    _CACHE["ddt"] = np.ascontiguousarray(dd.transpose(0, 2, 1).reshape(H0, 2, 128, 256))
    _CACHE["qd"] = np.ascontiguousarray(qd.reshape(H0, 2, 128, 1))
    _CACHE["kd"] = np.ascontiguousarray(kd.reshape(H0, 2, 128, 1))
    _CACHE["bd"] = np.ascontiguousarray(np.broadcast_to(bd[:, None, None], (H0, 64, 1)).copy())
    _CACHE["rope0"] = _host_rope(0)
    _CACHE["rope1"] = _host_rope(2048)
    R = np.zeros((128, 128), np.float32)
    for d_ in range(64):
        R[d_, 64 + d_] = -1.0
        R[64 + d_, d_] = 1.0
    _CACHE["rot"] = np.ascontiguousarray(R.T)
    _CACHE["cix4096"] = np.broadcast_to(np.arange(4096, dtype=np.float32)[None, :], (128, 4096)).copy()
    _CACHE["cix2560"] = np.broadcast_to(np.arange(2560, dtype=np.float32)[None, :], (128, 2560)).copy()
    _CACHE["prow"] = np.arange(128, dtype=np.float32).reshape(128, 1)
    _CACHE["idm"] = np.eye(128, dtype=np.float32)
    _CACHE["e2m"] = np.kron(np.eye(2, dtype=np.float32), np.ones((1, 64), np.float32))
    return _CACHE


def _stg_lhsT(w, H=2048):
    n = w.shape[0] // 128
    nh = H // 128
    return np.ascontiguousarray(w.reshape(n, 128, nh, 128).transpose(0, 3, 2, 1).reshape(n, 128, H))


def _stg_rhsT(w):
    return np.ascontiguousarray(w.reshape(4, 512, 16, 128).transpose(0, 3, 2, 1).reshape(4, 128, 8192))


def _stg_hO(w):
    no = w.shape[1] // 128
    return np.ascontiguousarray(w.reshape(16, 128, no, 128).transpose(2, 1, 0, 3).reshape(no, 128, 2048))


def _stg_hO_strips(w):
    return np.ascontiguousarray(w.reshape(16, 128, 4, 512).transpose(2, 1, 0, 3).reshape(4, 128, 8192))


def _pad_cp(cp):
    z = np.zeros(NCTRL + 2 + 2048, np.float32)
    z[2:2 + NCTRL] = cp
    return z


def build_program():
    C = _tables()
    nc = bass.Bass("TRN2", target_bir_lowering=False, detect_race_conditions=False)
    dt_ = nc.dram_tensor
    ein = lambda nm, shp: dt_(nm, list(shp), F32, kind="ExternalInput").ap()
    intl = lambda nm, shp: dt_(nm, list(shp), F32, kind="Internal").ap()

    xin = ein("xin", [2048, 2048])
    xout = dt_("xout", [2048, 2048], F32, kind="ExternalOutput").ap()
    xa = intl("xa", [2048, 2048])
    xb = intl("xb", [2048, 2048])

    l0_wq = ein("l0_wq", [48, 128, 2048])
    l0_wg = ein("l0_wg", [16, 128, 2048])
    l0_wo = ein("l0_wo", [4, 128, 8192])
    l0_nw = ein("l0_nw", [1, 2048])
    ddi = ein("ddi", [H0, 2, 128, 256])
    qdi = ein("qdi", [H0, 2, 128, 1])
    kdi = ein("kdi", [H0, 2, 128, 1])
    bdi = ein("bdi", [H0, 64, 1])
    iso = ein("iso", [1, 1])
    idm = ein("idm", [128, 128])
    e2m = ein("e2m", [2, 128])
    taps = ein("taps", [NSUP, 128, NTAP * 1024])
    cps = [ein(f"cp{i}", [NCTRL + 2 + 2048]) for i in range(6)]
    l1_wqkv = ein("l1_wqkv", [18, 128, 2048])
    l1_bqkv = ein("l1_bqkv", [18, 128, 1])
    l1_wp = ein("l1_wp", [4, 128, 8192])
    l1_bp = ein("l1_bp", [1, 2048])
    tb1 = ein("tb1", [128, 1])
    cix4096 = ein("cix4096", [128, 4096])
    l2_wq = ein("l2_wq", [16, 128, 2048])
    l2_wk = ein("l2_wk", [4, 128, 2048])
    l2_wv = ein("l2_wv", [4, 128, 2048])
    l2_wo = ein("l2_wo", [4, 128, 8192])
    l2_qn = ein("l2_qn", [128, 1])
    l2_kn = ein("l2_kn", [128, 1])
    cosq = ein("cosq", [128, 2048])
    sinq = ein("sinq", [128, 2048])
    rot = ein("rot", [128, 128])
    cix2560 = ein("cix2560", [128, 2560])
    prow = ein("prow", [128, 1])
    l2_w1 = ein("l2_w1", [44, 128, 2048])
    l2_w2 = ein("l2_w2", [44, 128, 2048])
    l2_w3 = ein("l2_w3", [44, 128, 2048])
    l3_win = ein("l3_win", [32, 128, 2048])
    l3_wg = ein("l3_wg", [32, 128, 2048])
    l3_wo = ein("l3_wo", [32, 128, 2048])
    l3_cw = ein("l3_cw", [32, 128, 4])
    l3_cb = ein("l3_cb", [32, 128, 1])
    l3_dtw = ein("l3_dtw", [32, 128, 2])
    l3_dtb = ein("l3_dtb", [32, 2, 1])
    l3_alog = ein("l3_alog", [32, 128, 1])
    l3_bp = ein("l3_bp", [32, 128, 1])
    l3_cp = ein("l3_cp", [32, 128, 1])
    l3_dp = ein("l3_dp", [32, 128, 1])
    l3f_w = ein("l3f_w", [4, 128, 8192])

    sdr = [intl(f"s{i}", [128, NQ]) for i in range(6)]
    qkv_scr = intl("qkv_scr", [128, 48 * 2048])
    att_scr = intl("att_scr", [2048, 2048])
    q_scr = intl("q_scr", [128, 16 * 2048])
    k_scr = intl("k_scr", [128, 4 * 2048])
    v_scr = intl("v_scr", [128, 4 * 2048])
    l2q = [intl(f"l2q{i}", [44, 128, 2048]) for i in range(3)]
    l3wq = intl("l3wq", [4, 128, 8192])
    p3 = intl("p3_proj", [128, 32 * 2048])
    st3 = intl("p3_st", [128, 32 * 2048])
    cu3 = intl("p3_cum", [128, 32 * 2048])
    xt3 = intl("p3_xt", [128, 32 * 2048])
    yg3 = intl("p3_yg", [128, 32 * 2048])
    f3 = intl("p3_f", [128, 16 * 2048])
    ag = {}
    for nm, wsrc, wdst in [("ag0", 2048, 2048), ("ag1", 4096, 4096), ("ag2", 4096, 4096),
                           ("ag3a", 96, 96), ("ag3b", 32, 32)]:
        ag[nm] = (intl(nm + "s", [128, wsrc]), dt_(nm + "d", [256, wdst], F32, kind="Internal").ap())

    with contextlib.ExitStack() as stack:
        ctx = Ctx(nc, stack)
        ident = ctx.sbuf("ident", [128, 128])
        ctx.load(ident[:, :], idm, q="y", sem="boot")
        ctx.dwait("boot", "t")
        ctx.dwait("boot", "v")
        emit_consts(ctx)
        emit_spline_recon(ctx, cps, taps, C["w0"], sdr, ident)
        emit_block_entry(ctx)
        emit_l0attn(ctx, xin, xa, l0_wq, l0_wg, l0_wo, l0_nw, ddi, qdi, kdi, bdi,
                       qkv_scr, att_scr, ag["ag0"][0], ag["ag0"][1], ident, iso)
        emit_block_entry(ctx)
        emit_spline_ffn(ctx, xa, xb, sdr[0], sdr[1], sdr[2], ident)
        emit_block_entry(ctx)
        emit_l1attn(ctx, xb, xa, l1_wqkv, l1_bqkv, l1_wp, l1_bp,
                       q_scr, att_scr, ag["ag1"][0], ag["ag1"][1], ident, tb1, cix4096)
        emit_block_entry(ctx)
        emit_spline_ffn(ctx, xa, xb, sdr[3], sdr[4], sdr[5], ident)
        emit_block_entry(ctx)
        emit_l2attn(ctx, xb, xa, l2_wq, l2_wk, l2_wv, l2_wo, l2_qn, l2_kn,
                       q_scr, k_scr, v_scr, att_scr, ag["ag2"][0], ag["ag2"][1], ident, iso,
                       cosq, sinq, rot, cix2560, prow)
        emit_block_entry(ctx)
        emit_l2ffn(ctx, xa, xb, l2_w1, l2_w2, l2_w3,
                      l2q[0], l2q[1], l2q[2], ident)
        emit_block_entry(ctx)
        emit_l3attn(ctx, xb, xa, l3_win, l3_wg, l3_wo, l3_cw, l3_cb, l3_dtw, l3_dtb,
                       l3_alog, l3_bp, l3_cp, l3_dp,
                       p3, st3, cu3, xt3, yg3, f3,
                       ag["ag3a"][0], ag["ag3a"][1], ag["ag3b"][0], ag["ag3b"][1], ident, iso, e2m)
        emit_block_entry(ctx)
        emit_l3ffn(ctx, xa, xout, [l3f_w[i] for i in range(4)],
                      [l3wq[i] for i in range(4)], ident)
        ctx.dwait("ln_xdst", "g")
        ctx.dwait("ln_xdst", "v")
    return nc


def kernel(**inputs):
    C = _tables()
    if "nc" not in _CACHE:
        _CACHE["nc"] = build_program()
    nc = _CACHE["nc"]
    x0 = np.ascontiguousarray(np.asarray(inputs["L0_attn_hidden_states"], np.float32))
    B, S, D = x0.shape
    common = dict(
        l0_wq=_stg_lhsT(np.asarray(inputs["L0_attn_qkv_w"], np.float32)),
        l0_wg=_stg_lhsT(np.asarray(inputs["L0_attn_gate_w"], np.float32)),
        l0_wo=_stg_rhsT(np.asarray(inputs["L0_attn_out_w"], np.float32)),
        l0_nw=np.asarray(inputs["L0_attn_norm_w"], np.float32).reshape(1, 2048),
        ddi=C["ddt"], qdi=C["qd"], kdi=C["kd"], bdi=C["bd"],
        idm=C["idm"], e2m=C["e2m"], taps=C["taps"],
        cp0=_pad_cp(np.asarray(inputs["L0_ffn_gate_cp"], np.float32)),
        cp1=_pad_cp(np.asarray(inputs["L0_ffn_up_cp"], np.float32)),
        cp2=_pad_cp(np.asarray(inputs["L0_ffn_down_cp"], np.float32)),
        cp3=_pad_cp(np.asarray(inputs["L1_ffn_gate_cp"], np.float32)),
        cp4=_pad_cp(np.asarray(inputs["L1_ffn_up_cp"], np.float32)),
        cp5=_pad_cp(np.asarray(inputs["L1_ffn_down_cp"], np.float32)),
        l1_wqkv=_stg_hO(np.asarray(inputs["L1_attn_W_qkv"], np.float32)),
        l1_bqkv=np.ascontiguousarray(np.asarray(inputs["L1_attn_b_qkv"], np.float32).reshape(18, 128, 1)),
        l1_wp=_stg_hO_strips(np.asarray(inputs["L1_attn_W_proj"], np.float32)),
        l1_bp=np.asarray(inputs["L1_attn_b_proj"], np.float32).reshape(1, 2048),
        cix4096=C["cix4096"],
        l2_wq=_stg_lhsT(np.asarray(inputs["L2_attn_Wq"], np.float32)),
        l2_wk=_stg_lhsT(np.asarray(inputs["L2_attn_Wk"], np.float32)),
        l2_wv=_stg_lhsT(np.asarray(inputs["L2_attn_Wv"], np.float32)),
        l2_wo=_stg_rhsT(np.asarray(inputs["L2_attn_Wo"], np.float32)),
        l2_qn=np.asarray(inputs["L2_attn_qn_w"], np.float32).reshape(128, 1),
        l2_kn=np.asarray(inputs["L2_attn_kn_w"], np.float32).reshape(128, 1),
        rot=C["rot"], cix2560=C["cix2560"], prow=C["prow"],
        l2_w1=stage_l2w(np.asarray(inputs["L2_ffn_w1"], np.float32)),
        l2_w2=stage_l2w(np.asarray(inputs["L2_ffn_w2"], np.float32)),
        l2_w3=stage_l2w3(np.asarray(inputs["L2_ffn_w3"], np.float32)),
        l3_win=_stg_lhsT(np.asarray(inputs["L3_attn_in_proj_w"], np.float32)),
        l3_wg=_stg_lhsT(np.asarray(inputs["L3_attn_gate_proj_w"], np.float32)),
        l3_wo=np.ascontiguousarray(np.asarray(inputs["L3_attn_out_proj_w"], np.float32).T.reshape(32, 128, 2048)),
        l3_cw=np.ascontiguousarray(np.asarray(inputs["L3_attn_conv_w"], np.float32).reshape(32, 128, 4)),
        l3_cb=np.ascontiguousarray(np.asarray(inputs["L3_attn_conv_b"], np.float32).reshape(32, 128, 1)),
        l3_alog=np.ascontiguousarray(np.asarray(inputs["L3_attn_A_log"], np.float32).reshape(-1).reshape(32, 128, 1)),
        l3_bp=np.ascontiguousarray(np.asarray(inputs["L3_attn_Bp"], np.float32).reshape(-1).reshape(32, 128, 1)),
        l3_cp=np.ascontiguousarray(np.asarray(inputs["L3_attn_Cp"], np.float32).reshape(-1).reshape(32, 128, 1)),
        l3_dp=np.ascontiguousarray(np.asarray(inputs["L3_attn_Dp"], np.float32).reshape(-1).reshape(32, 128, 1)),
        l3f_w=stage_w_strips(np.asarray(inputs["L3_ffn_weight"], np.float32), 512),
    )
    dtw = np.asarray(inputs["L3_attn_dt_w"], np.float32)
    dtb = np.asarray(inputs["L3_attn_dt_b"], np.float32)
    dtw_st = np.zeros((32, 128, 2), np.float32)
    dtb_st = np.zeros((32, 2, 1), np.float32)
    for cc in range(32):
        for g_ in range(2):
            dtw_st[cc, g_ * 64:(g_ + 1) * 64, g_] = dtw[2 * cc + g_]
        dtb_st[cc, 0, 0] = dtb[2 * cc]
        dtb_st[cc, 1, 0] = dtb[2 * cc + 1]
    common["l3_dtw"] = dtw_st
    common["l3_dtb"] = dtb_st

    in_maps = []
    for c in range(8):
        seq, half = c // 2, c % 2
        d = dict(common)
        d["xin"] = np.ascontiguousarray(x0[seq, half * 2048:(half + 1) * 2048, :])
        d["iso"] = np.array([[float(half)]], np.float32)
        d["tb1"] = (half * 2048 + np.arange(128, dtype=np.float32)).reshape(128, 1)
        cq, sq = C["rope1"] if half else C["rope0"]
        d["cosq"] = cq
        d["sinq"] = sq
        in_maps.append(d)
    res = run_bass_kernel_spmd(nc, in_maps, core_ids=list(range(8)))
    out = np.empty((B, S, D), np.float32)
    for c in range(8):
        seq, half = c // 2, c % 2
        out[seq, half * 2048:(half + 1) * 2048, :] = res.results[c]["xout"]
    return out
